# version 26
# baseline (speedup 1.0000x reference)
import contextlib

import numpy as np

import concourse.bass as bass
import concourse.mybir as mybir
from concourse.alu_op_type import AluOpType as OP
from concourse.bass_utils import run_bass_kernel_spmd

F32 = mybir.dt.float32
BF16 = mybir.dt.bfloat16
AF = mybir.ActivationFunctionType

RC = 12582912.0
DEBUG = 0

T = 2048
D = 1024
DIN = 2048
NST = 128
H = 32
PH = 64
CONVD = 2304
XBCDT = 2336
NPROJ = 4384
DFF = 4096
SEG = 256
NSEG = T // SEG
CH = 128
SEG2 = 512
EPS_LN = 1e-5
EPS_RMS = 1e-6

ENGS = ("tensor", "vector", "scalar", "gpsimd", "sync")


class Emitter:

    def __init__(self, nc):
        self.nc = nc
        self.recs = {e: [] for e in ENGS}
        self.cnt = {e: 0 for e in ENGS}
        self.dcnt = {}
        self.dma_recs = {}
        self.buf = {}
        self.watermark = {e: {} for e in ENGS}
        self.sem_handles = {}
        self.targets = {}

    def _wait(self, eng, prod_key, val):
        if val <= 0:
            return
        wm = self.watermark[eng]
        if wm.get(prod_key, 0) >= val:
            return
        wm[prod_key] = val
        self.targets.setdefault(prod_key, set()).add(val)
        self.recs[eng].append({"k": "wait", "pk": prod_key, "v": val})

    def _pre(self, eng, reads, writes):
        me = ("e", eng)
        for b in reads:
            st = self.buf.get(b)
            if st and st["w"]:
                if st["w"][0] != me:
                    self._wait(eng, st["w"][0], st["w"][1])
                elif self.cnt[eng] - st["w"][1] <= 2:
                    self._wait(eng, st["w"][0], st["w"][1])
        for b in writes:
            st = self.buf.get(b)
            if st:
                if st["w"]:
                    if st["w"][0] != me:
                        self._wait(eng, st["w"][0], st["w"][1])
                    elif self.cnt[eng] - st["w"][1] <= 2:
                        self._wait(eng, st["w"][0], st["w"][1])
                for rk, rv in st["r"].items():
                    if rk != me:
                        self._wait(eng, rk, rv)

    def _post(self, kind, key, count, reads, writes):
        k = (kind, key)
        for b in writes:
            self.buf[b] = {"w": (k, count), "r": {}}
        for b in reads:
            st = self.buf.setdefault(b, {"w": None, "r": {}})
            st["r"][k[0:2]] = count

    def op(self, eng, fn, reads=(), writes=()):
        self._pre(eng, reads, writes)
        self.cnt[eng] += 1
        self.recs[eng].append({"k": "op", "fn": fn, "c": self.cnt[eng]})
        self._post("e", eng, self.cnt[eng], reads, writes)

    def dma(self, eng, stream, out, in_, reads=(), writes=(), transpose=False,
            slow=False):
        self._pre(eng, reads, writes)
        self.dcnt[stream] = self.dcnt.get(stream, 0) + 16
        self.recs[eng].append({"k": "dma", "out": out, "in": in_, "t": transpose,
                               "s": slow, "stream": stream, "c": self.dcnt[stream]})
        self._post("d", "dma_" + stream, self.dcnt[stream], reads, writes)

    def raw(self, eng, fn):
        self.recs[eng].append({"k": "raw", "fn": fn})

    def phase_barrier(self):
        for eng in ENGS:
            for e2 in ENGS:
                if e2 != eng and self.cnt[e2] > 0:
                    self._wait(eng, ("e", e2), self.cnt[e2])
            for s, v in self.dcnt.items():
                self._wait(eng, ("d", "dma_" + s), v)

    def finalize_sems(self, stack):
        for e in ENGS:
            self.sem_handles[e] = stack.enter_context(self.nc.semaphore(f"s_{e}"))
        for s in self.dcnt:
            self.sem_handles["dma_" + s] = stack.enter_context(
                self.nc.semaphore(f"sd_{s}")
            )
        self.cmap = {}
        for pk, vals in self.targets.items():
            self.cmap[pk] = sorted(vals)

    def _remap(self, pk, v):
        import bisect
        if pk[0] == "d":
            return v
        lst = self.cmap[pk]
        return bisect.bisect_left(lst, v) + 1

    def replay(self, eng, e):
        import bisect
        for r in self.recs[eng]:
            k = r["k"]
            if k == "wait":
                pk = r["pk"]
                sem = self.sem_handles[pk[1]]
                e.wait_ge(sem, self._remap(pk, r["v"]))
            elif k == "op":
                inst = r["fn"](e)
                pk = ("e", eng)
                lst = self.cmap.get(pk)
                if lst and r["c"] in self.targets.get(pk, ()):
                    inst.then_inc(self.sem_handles[eng], 1)
            elif k == "dma":
                if r["t"]:
                    inst = e.dma_start_transpose(out=r["out"], in_=r["in"])
                elif r["s"]:
                    inst = e.dma_start(out=r["out"], in_=r["in"],
                                       allow_slow_non_contiguous=True)
                else:
                    inst = e.dma_start(out=r["out"], in_=r["in"])
                inst.then_inc(self.sem_handles["dma_" + r["stream"]], 16)
            else:
                r["fn"](e)


def _col(t, i):
    return t[:, i : i + 1]


class Build:
    def __init__(self, nc, stop_after=None):
        self.nc = nc
        self.stop_after = stop_after
        self.stack = contextlib.ExitStack()
        self.em = Emitter(nc)
        self._declare_dram()
        self._declare_sbuf()
        self._emit_program()
        self._assemble()

    def _declare_dram(self):
        nc = self.nc
        self.dx = nc.dram_tensor("x", [T, D], F32, kind="ExternalInput").ap()
        self.d_norm_w = nc.dram_tensor("norm_w", [1, D], F32, kind="ExternalInput").ap()
        self.d_win = nc.dram_tensor("win_t", [D, NPROJ], F32, kind="ExternalInput").ap()
        self.d_conv_w = nc.dram_tensor("conv_w", [CONVD, 4], F32, kind="ExternalInput").ap()
        self.d_conv_b = nc.dram_tensor("conv_b", [CONVD, 1], F32, kind="ExternalInput").ap()
        self.d_alog = nc.dram_tensor("a_log", [H, 1], F32, kind="ExternalInput").ap()
        self.d_dp = nc.dram_tensor("dp", [1, H], F32, kind="ExternalInput").ap()
        self.d_dtb = nc.dram_tensor("dt_bias", [H, 1], F32, kind="ExternalInput").ap()
        self.d_onw = nc.dram_tensor("out_norm_w", [1, DIN], F32, kind="ExternalInput").ap()
        self.d_wout = nc.dram_tensor("wout_t", [DIN, D], F32, kind="ExternalInput").ap()
        self.d_w1 = nc.dram_tensor("w1_t", [D, DFF], F32, kind="ExternalInput").ap()
        self.d_b1 = nc.dram_tensor("b1", [DFF, 1], F32, kind="ExternalInput").ap()
        self.d_w2 = nc.dram_tensor("w2_t", [DFF, D], F32, kind="ExternalInput").ap()
        self.d_b2 = nc.dram_tensor("b2", [1, D], F32, kind="ExternalInput").ap()
        self.d_out = nc.dram_tensor("out", [T, D], F32, kind="ExternalOutput").ap()

        ik = "ExternalOutput" if DEBUG else "Internal"
        self.d_xmid = nc.dram_tensor("xmid", [T, D], F32, kind=ik).ap()
        self.d_h = nc.dram_tensor("hbuf", [DFF, T], F32, kind=ik).ap()
        self.d_w1q = nc.dram_tensor("w1q_spill", [D, DFF], BF16).ap()
        self.d_w2q = nc.dram_tensor("w2q_spill", [DFF, D], BF16).ap()
        self.d_bn = nc.dram_tensor("bounce", [64, 512], F32).ap()
        self.d_cc1i = nc.dram_tensor("cc1i", [1, 1], F32).ap()
        self.d_cc1o = nc.dram_tensor("cc1o", [1, 1], F32).ap()
        self.d_cc2i = nc.dram_tensor("cc2i", [1, 1], F32).ap()
        self.d_cc2o = nc.dram_tensor("cc2o", [1, 1], F32).ap()
        self.ntap = 0
        if DEBUG:
            self.d_dbg = nc.dram_tensor("dbg", [128, 64 * 512], F32, kind="ExternalOutput").ap()

    def _sb(self, name, p, f, dt=F32):
        return self.stack.enter_context(self.nc.sbuf_tensor(name, [p, f], dt))

    def _declare_sbuf(self):
        sb = self._sb
        self.wq_in = sb("wq_in", 128, 8 * NPROJ, BF16)
        self.wq_out = sb("wq_out", 128, 16 * D, BF16)
        self.arf = sb("arf", 128, 22720)
        a = self.arf
        o = 0

        def take(n):
            nonlocal o
            v = a[:, o : o + n]
            o += n
            return v

        x_off = o
        self.x_sb = take(2 * D)
        self.g_sb = take(DIN)
        self.x4 = self.arf[:, x_off : x_off + 4 * D]
        self.xbc = take(19 * (SEG + 4))
        self.u_tok = take(DIN // 2)
        self.bcst = take(2048)
        self.y_sb = take(DIN)
        self.state = take(DIN)
        self.cvs = take(SEG)
        self.uD = take(512)
        self.btok = take(NST)
        self.cbt = take(CH)
        self.mask01 = take(CH)
        self.ident = take(128)
        self.bc_w = take(D)
        self.bc_w2 = take(DIN)
        self.small = take(512)
        self.consts = take(16)
        self.convw = take(96)
        self.cvt = take(64)
        self.ffcol = take(48)
        self.ffs = take(16)
        self.dcol = take(4 * H)
        self.bcinvs = take(SEG)
        self.x_ab = take(D)
        assert o <= 22720, o
        self.dtp = self._sb("dtp", 32, 9 * SEG)
        self.arb = sb("arb", 128, 4096, BF16)
        b = self.arb
        self.aqT = b[:, 0:2048]
        self.yqT = b[:, 2048:4096]
        self.aq1T = b[:, 0:4096]
        self.hq = b[:, 0:4096]

        ps = lambda n: self.stack.enter_context(self.nc.psum_tensor(n, [128, 512], F32))
        self.ps_a = ps("ps_a")
        self.ps_z = ps("ps_z")
        self.ps_c = ps("ps_c")
        self.ps_y = ps("ps_y")
        self.ps_s = ps("ps_s")
        self.ps_t = ps("ps_t")
        self.ps_o = ps("ps_o")
        self.ps_o2 = ps("ps_o2")

    def bcast_row(self, stream, out2d, dram_row, ncols, name, reads=()):
        self.em.dma(
            "sync",
            stream,
            out2d.rearrange("p (a f) -> p a f", a=1),
            dram_row.unsqueeze(0).to_broadcast((128, 1, ncols)),
            reads=reads,
            writes=(name,),
        )

    def tiny(self, eng, fn, reads, writes):
        self.em.op(eng, fn, reads, writes)

    def _emit_program(self):
        em = self.em
        nc = self.nc

        em.op("gpsimd", lambda e: e.memset(_col(self.consts, 3), RC), writes=("rc_col",))
        em.op(
            "gpsimd",
            lambda e: e.iota(
                self.ident, pattern=[[1, 128]], base=0, channel_multiplier=-1,
                allow_small_or_imprecise_dtypes=True,
            ),
            writes=("ident",),
        )
        em.op(
            "vector",
            lambda e: e.tensor_scalar(
                out=self.mask01, in0=self.ident, scalar1=0.0, scalar2=None, op0=OP.is_ge
            ),
            reads=("ident",),
            writes=("mask01",),
        )
        em.op(
            "vector",
            lambda e: e.tensor_scalar(
                out=self.ident, in0=self.ident, scalar1=0.0, scalar2=None, op0=OP.is_equal
            ),
            reads=("ident",),
            writes=("ident",),
        )
        em.dma("sync", "b", self.bc_w.rearrange("p (a f) -> p a f", a=1),
               self.d_norm_w.unsqueeze(0).to_broadcast((128, 1, D)), writes=("bc_w",))
        em.dma("sync", "b", self.bc_w2.rearrange("p (a f) -> p a f", a=1),
               self.d_onw.unsqueeze(0).to_broadcast((128, 1, DIN)), writes=("bc_w2",))
        em.dma("sync", "b", self.consts[0:H, 0:1], self.d_alog, writes=("consts",))
        em.dma("sync", "b", self.consts[0:H, 1:2], self.d_dtb, writes=("consts",))
        em.op("scalar",
              lambda e: e.activation(out=self.consts[0:H, 2:3], in_=self.consts[0:H, 0:1], func=AF.Exp),
              reads=("consts",), writes=("consts",))
        em.op("vector",
              lambda e: e.tensor_scalar(out=self.consts[0:H, 2:3], in0=self.consts[0:H, 2:3],
                                        scalar1=-1.0, scalar2=None, op0=OP.mult),
              reads=("consts",), writes=("consts",))
        em.dma("sync", "b", self.convw[:, 0:72].rearrange("p (c k) -> p c k", c=18),
               self.d_conv_w.rearrange("(c p) k -> p c k", p=128), writes=("convw",))
        em.dma("sync", "b", self.convw[:, 72:90].rearrange("p (c k) -> p c k", k=1),
               self.d_conv_b.rearrange("(c p) k -> p c k", p=128), writes=("convw",),
               slow=True)
        em.dma("sync", "b", self.small[:, 352 : 352 + H].rearrange("p (a h) -> p a h", a=1),
               self.d_dp.unsqueeze(0).to_broadcast((128, 1, H)), writes=("smallDp",))
        em.dma("sync", "b", self.small[:, 320 : 320 + 32].rearrange("p (n k) -> p n k", k=1),
               self.d_b1.rearrange("(n p) k -> p n k", p=128), writes=("smallB1",),
               slow=True)

        self._quant_weights()
        if self.stop_after == "weights":
            self._final_drain()
            return

        self._attn_phase()
        if self.stop_after == "attn":
            self._final_drain()
            return

        self._ffn_phase()
        self._final_drain()

    def _final_drain(self):
        em = self.em
        em.phase_barrier()

    def _quant_weights(self):
        em = self.em
        sm = self.small

        slots = [
            (self.bcst[:, 0:2048], "wsl0"),
            (self.y_sb[:, 0:2048], "wsl1"),
            (self.x_sb[:, 0:2048], "wsl2"),
            (self.g_sb[:, 0:2048], "wsl3"),
            (self.xbc[:, 0:2048], "wsl4"),
            (self.xbc[:, 2048:4096], "wsl5"),
            (self.state[:, 0:2048], "wsl6"),
        ]
        gidx = [0]

        def nxt():
            i = gidx[0]
            gidx[0] += 1
            sl, sname = slots[i % 7]
            return sl, sname, f"wld{i % 7}", "sync"

        def tiles(K, N):
            for k in range(K // 128):
                for c in range((N + 2047) // 2048):
                    yield k, c, min(2048, N - c * 2048)

        def stat_pass(dram, K, N, mode, acc_base):
            idx = 0
            for k, c, cn in tiles(K, N):
                sl, sname, st, seng = nxt()
                stg = sl[:, 0:cn]
                em.dma(seng, st, stg,
                       dram[k * 128 : (k + 1) * 128, c * 2048 : c * 2048 + cn],
                       writes=(sname,))
                accc = _col(sm, acc_base + idx)
                if mode == "mean":
                    em.op("scalar",
                          lambda e, stg=stg, accc=accc: e.activation(
                              out=stg, in_=stg, func=AF.Abs, accum_out=accc),
                          reads=(sname,), writes=(sname, "smallacc"))
                else:
                    em.op("vector",
                          lambda e, stg=stg, accc=accc: e.tensor_reduce(
                              out=accc, in_=stg, axis=mybir.AxisListType.X, op=OP.max,
                              apply_absolute_value=True),
                          reads=(sname,), writes=("smallacc",))
                idx += 1
            return idx

        def fold_cols(nidx, acc_base, mode, out_col, fi):
            op = OP.add if mode == "mean" else OP.max
            red = _col(sm, acc_base + nidx)
            em.op("vector",
                  lambda e: e.tensor_reduce(out=red, in_=sm[:, acc_base : acc_base + nidx],
                                            axis=mybir.AxisListType.X, op=op),
                  reads=("smallacc",), writes=(f"smallred{fi}",))
            self._fold_partitions(red, f"smallred{fi}", op, acc_base + nidx + 1,
                                  f"smallfold{fi}")
            em.dma("gpsimd", "o", self.d_bn[0:1, fi : fi + 1],
                   sm[0:1, acc_base + nidx + 1 : acc_base + nidx + 2],
                   reads=(f"smallfold{fi}",), writes=(f"bnw{fi}",))
            self.bcast_row("b", _col(sm, out_col), self.d_bn[0:1, fi : fi + 1], 1,
                           f"smallc{out_col}", reads=(f"bnw{fi}",))

        def round_pass(dram, K, N, inv_col, dest_fn, clip1, spill=None):
            idx = 0
            for k, c, cn in tiles(K, N):
                sl, sname, st, seng = nxt()
                stg = sl[:, 0:cn]
                em.dma(seng, st, stg,
                       dram[k * 128 : (k + 1) * 128, c * 2048 : c * 2048 + cn],
                       writes=(sname,))
                em.op("scalar",
                      lambda e, stg=stg: e.activation(
                          out=stg, in_=stg, func=AF.Identity,
                          scale=_col(sm, inv_col), bias=_col(self.consts, 3)),
                      reads=(sname, f"smallc{inv_col}", "rc_col"), writes=(sname,))
                dest, dname = dest_fn(k, c, cn, idx)
                veng = "vector" if idx % 2 == 0 else "gpsimd"
                if clip1:
                    em.op(veng,
                          lambda e, stg=stg, dest=dest: e.tensor_scalar(
                              out=dest, in0=stg, scalar1=-RC, scalar2=1.0,
                              op0=OP.add, op1=OP.min),
                          reads=(sname,), writes=(dname,))
                    em.op(veng,
                          lambda e, dest=dest: e.tensor_scalar(
                              out=dest, in0=dest, scalar1=-1.0, scalar2=None, op0=OP.max),
                          reads=(dname,), writes=(dname,))
                else:
                    em.op(veng,
                          lambda e, stg=stg, dest=dest: e.tensor_scalar(
                              out=dest, in0=stg, scalar1=-RC, scalar2=None, op0=OP.add),
                          reads=(sname,), writes=(dname,))
                if spill is not None:
                    em.dma("gpsimd", f"osp{idx % 2}",
                           spill[k * 128 : (k + 1) * 128, c * 2048 : c * 2048 + cn],
                           dest, reads=(dname,))
                idx += 1

        def scalar_ops_mean(sum_col, KN, out_inv_col):
            c = _col(sm, sum_col)
            em.op("vector",
                  lambda e: e.tensor_scalar(out=c, in0=c, scalar1=1.0 / KN, scalar2=1e-5,
                                            op0=OP.mult, op1=OP.max),
                  reads=(f"smallc{sum_col}",), writes=(f"smallc{sum_col}",))
            ic = _col(sm, out_inv_col)
            em.op("vector", lambda e: e.reciprocal(out=ic, in_=c),
                  reads=(f"smallc{sum_col}",), writes=(f"smallc{out_inv_col}",))

        def scalar_ops_max(max_col, out_inv_col, sw_col):
            c = _col(sm, max_col)
            em.op("vector",
                  lambda e: e.tensor_scalar(out=_col(sm, sw_col), in0=c, scalar1=1e-6,
                                            scalar2=1.0 / 127.0, op0=OP.max, op1=OP.mult),
                  reads=(f"smallc{max_col}",), writes=(f"smallc{sw_col}",))
            em.op("vector", lambda e: e.reciprocal(out=_col(sm, out_inv_col), in_=_col(sm, sw_col)),
                  reads=(f"smallc{sw_col}",), writes=(f"smallc{out_inv_col}",))

        def dest_spill(k, c, cn, idx):
            return self.arb[:, (idx % 2) * 2048 : (idx % 2) * 2048 + cn], f"wqs{idx % 2}"

        def dest_win(k, c, cn, idx):
            return self.wq_in[:, k * NPROJ + c * 2048 : k * NPROJ + c * 2048 + cn], "wq_in"

        def dest_wout(k, c, cn, idx):
            return self.wq_out[:, k * D + c * 2048 : k * D + c * 2048 + cn], "wq_out"

        n1 = stat_pass(self.d_w1, D, DFF, "max", 0)
        n2 = stat_pass(self.d_w2, DFF, D, "max", 60)
        fold_cols(n1, 0, "max", 40, fi=0)
        scalar_ops_max(40, 41, 42)
        n3 = stat_pass(self.d_win, D, NPROJ, "mean", 100)
        fold_cols(n2, 60, "max", 43, fi=1)
        scalar_ops_max(43, 44, 45)
        n4 = stat_pass(self.d_wout, DIN, D, "mean", 130)
        fold_cols(n3, 100, "mean", 46, fi=2)
        scalar_ops_mean(46, D * NPROJ, 47)
        fold_cols(n4, 130, "mean", 48, fi=3)
        scalar_ops_mean(48, DIN * D, 49)
        round_pass(self.d_w1, D, DFF, 41, dest_spill, False, spill=self.d_w1q)
        round_pass(self.d_w2, DFF, D, 44, dest_spill, False, spill=self.d_w2q)
        round_pass(self.d_win, D, NPROJ, 47, dest_win, clip1=True)
        round_pass(self.d_wout, DIN, D, 49, dest_wout, clip1=True)

    def tap(self, ap, reads):
        if not DEBUG or self.ntap >= 64:
            return
        i = self.ntap
        self.ntap += 1
        p, w = ap.shape[0], ap.shape[-1]
        self.em.phase_barrier()
        self.em.dma("gpsimd", "o", self.d_dbg[0:p, i * 512 : i * 512 + w], ap, reads=reads,
                    slow=True)
        self.em.phase_barrier()

    def _fold_partitions(self, col, rname, op, tmp_i, out_name):
        em = self.em
        em.op("tensor",
              lambda e: e.matmul(self.ps_c[0:1, 0:128], col, self.ident, is_transpose=True),
              reads=(rname, "ident"), writes=("ps_c",))
        em.op("vector",
              lambda e: e.tensor_reduce(out=self.small[0:1, tmp_i : tmp_i + 1],
                                        in_=self.ps_c[0:1, 0:128],
                                        axis=mybir.AxisListType.X, op=op),
              reads=("ps_c",), writes=(out_name,))

    def _pe_round_T(self, src, srcnames, nk, dst_fn, dstname):
        em = self.em
        banks = ((self.ps_c, "ps_c"), (self.ps_t, "ps_t"),
                 (self.ps_a, "ps_a"), (self.ps_z, "ps_z"))
        for k in range(nk):
            pst, pname = banks[k % 4]
            em.op("tensor",
                  lambda e, k=k, pst=pst: e.matmul(
                      pst[:, 0:128], src[:, k * 128 : (k + 1) * 128], self.ident,
                      is_transpose=True),
                  reads=tuple(srcnames) + ("ident",), writes=(pname,))
            em.op("vector",
                  lambda e, k=k, pst=pst: e.tensor_scalar(
                      out=dst_fn(k), in0=pst[:, 0:128], scalar1=RC, scalar2=-RC,
                      op0=OP.add, op1=OP.add),
                  reads=(pname,), writes=(dstname,))

    def _rsqrt_col(self, out_c, in_c, eps, rnames, wname, dbg=False, tmp=120):
        em = self.em
        sm = self.small
        t_c = _col(sm, tmp)
        r_c = out_c
        em.op("vector",
              lambda e: e.tensor_scalar(out=t_c, in0=in_c, scalar1=eps, scalar2=None, op0=OP.add),
              reads=rnames, writes=(f"c{tmp}",))
        em.op("scalar", lambda e: e.activation(out=r_c, in_=t_c, func=AF.Sqrt),
              reads=(f"c{tmp}",), writes=(wname,))
        em.op("vector", lambda e: e.reciprocal(out=r_c, in_=r_c),
              reads=(wname,), writes=(wname,))

    def _ln_quant_prep(self, xi, scr1, scr2, scr1n, scr2n, xname, aq_dst, aq_name, cbase,
                       dbg=False, tpose=None, rtmp=120):
        em = self.em
        sm = self.small
        c = lambda i: _col(sm, cbase + i)
        cn = lambda i: f"c{cbase + i}"

        em.op("vector",
              lambda e: e.tensor_reduce(out=c(0), in_=xi, axis=mybir.AxisListType.X, op=OP.add),
              reads=(xname,), writes=(cn(0),))
        em.op("scalar",
              lambda e: e.activation(out=scr1, in_=xi, func=AF.Square, accum_out=c(1)),
              reads=(xname,), writes=(scr1n, cn(1)))
        em.op("vector", lambda e: e.tensor_scalar(out=c(0), in0=c(0), scalar1=1.0 / D,
                                                  scalar2=None, op0=OP.mult),
              reads=(cn(0),), writes=(cn(0),))
        em.op("scalar", lambda e: e.activation(out=c(2), in_=c(0), func=AF.Square),
              reads=(cn(0),), writes=(cn(2),))
        em.op("vector",
              lambda e: e.scalar_tensor_tensor(out=c(1), in0=sm[:, cbase + 1 : cbase + 2],
                                               scalar=1.0 / D, in1=c(2), op0=OP.mult,
                                               op1=OP.subtract),
              reads=(cn(1), cn(2)), writes=(cn(1),))
        if dbg:
            self.tap(self.small[:, cbase : cbase + 12], (cn(0), cn(1), cn(2)))
        self._rsqrt_col(c(3), c(1), EPS_LN, (cn(1),), cn(3), dbg=dbg, tmp=rtmp)
        if dbg:
            self.tap(self.small[:, cbase : cbase + 12], (cn(3),))
        em.op("scalar", lambda e: e.activation(out=c(4), in_=c(3), func=AF.Square),
              reads=(cn(3),), writes=(cn(4),))
        em.op("vector", lambda e: e.tensor_tensor(out=c(4), in0=c(4), in1=c(1), op=OP.mult),
              reads=(cn(4), cn(1)), writes=(cn(4),))
        self._rsqrt_col(c(5), c(4), EPS_RMS, (cn(4),), cn(5), tmp=rtmp)
        em.op("vector", lambda e: e.tensor_tensor(out=c(5), in0=c(5), in1=c(3), op=OP.mult),
              reads=(cn(5), cn(3)), writes=(cn(5),))
        em.op("vector",
              lambda e: e.scalar_tensor_tensor(out=c(6), in0=c(0), scalar=-1.0, in1=c(5),
                                               op0=OP.mult, op1=OP.mult),
              reads=(cn(0), cn(5)), writes=(cn(6),))
        em.op("scalar",
              lambda e: e.activation(out=scr1, in_=xi, func=AF.Identity, scale=c(5), bias=c(6)),
              reads=(xname, cn(5), cn(6)), writes=(scr1n,))
        em.op("vector",
              lambda e: e.scalar_tensor_tensor(out=scr2, in0=scr1, scalar=1.0, in1=self.bc_w,
                                               op0=OP.bypass, op1=OP.mult, accum_out=c(7)),
              reads=(scr1n, "bc_w"), writes=(scr2n, cn(7)))
        if dbg:
            self.tap(scr1[:, 0:512], (scr1n,))
            self.tap(scr2[:, 0:512], (scr2n,))
        em.op("scalar",
              lambda e: e.activation(out=scr1, in_=scr2, func=AF.Square, accum_out=c(8)),
              reads=(scr2n,), writes=(scr1n, cn(8)))
        em.op("vector", lambda e: e.tensor_scalar(out=c(7), in0=c(7), scalar1=1.0 / D,
                                                  scalar2=None, op0=OP.mult),
              reads=(cn(7),), writes=(cn(7),))
        em.op("scalar", lambda e: e.activation(out=c(2), in_=c(7), func=AF.Square),
              reads=(cn(7),), writes=(cn(2),))
        em.op("vector",
              lambda e: e.scalar_tensor_tensor(out=c(8), in0=sm[:, cbase + 8 : cbase + 9],
                                               scalar=1.0 / D, in1=c(2), op0=OP.mult,
                                               op1=OP.subtract),
              reads=(cn(8), cn(2)), writes=(cn(8),))
        self._rsqrt_col(c(9), c(8), EPS_LN, (cn(8),), cn(9), tmp=rtmp)
        em.op("vector",
              lambda e: e.tensor_reduce(out=c(2), in_=scr2, axis=mybir.AxisListType.X, op=OP.max),
              reads=(scr2n,), writes=(cn(2),))
        em.op("vector",
              lambda e: e.tensor_reduce(out=c(1), in_=scr2, axis=mybir.AxisListType.X, op=OP.min),
              reads=(scr2n,), writes=(cn(1),))
        em.op("vector", lambda e: e.tensor_tensor(out=c(2), in0=c(2), in1=c(7), op=OP.subtract),
              reads=(cn(2), cn(7)), writes=(cn(2),))
        em.op("vector", lambda e: e.tensor_tensor(out=c(1), in0=c(7), in1=c(1), op=OP.subtract),
              reads=(cn(1), cn(7)), writes=(cn(1),))
        em.op("vector", lambda e: e.tensor_tensor(out=c(2), in0=c(2), in1=c(1), op=OP.max),
              reads=(cn(2), cn(1)), writes=(cn(2),))
        em.op("vector", lambda e: e.tensor_tensor(out=c(2), in0=c(2), in1=c(9), op=OP.mult),
              reads=(cn(2), cn(9)), writes=(cn(2),))
        em.op("vector", lambda e: e.tensor_scalar(out=c(2), in0=c(2), scalar1=1e-5,
                                                  scalar2=None, op0=OP.max),
              reads=(cn(2),), writes=(cn(2),))
        em.op("vector", lambda e: e.tensor_scalar(out=c(10), in0=c(2), scalar1=1.0 / 127.0,
                                                  scalar2=None, op0=OP.mult),
              reads=(cn(2),), writes=(cn(10),))
        em.op("vector", lambda e: e.reciprocal(out=c(11), in_=c(10)),
              reads=(cn(10),), writes=(cn(11),))
        em.op("vector", lambda e: e.tensor_tensor(out=c(9), in0=c(9), in1=c(11), op=OP.mult),
              reads=(cn(9), cn(11)), writes=(cn(9),))
        em.op("vector",
              lambda e: e.scalar_tensor_tensor(out=c(6), in0=c(7), scalar=-1.0, in1=c(9),
                                               op0=OP.mult, op1=OP.mult),
              reads=(cn(7), cn(9)), writes=(cn(6),))
        em.op("scalar",
              lambda e: e.activation(out=scr1, in_=scr2, func=AF.Identity, scale=c(9), bias=c(6)),
              reads=(scr2n, cn(9), cn(6)), writes=(scr1n,))
        if tpose is not None:
            nk, dst_fn, dstname = tpose
            self._pe_round_T(scr1, (scr1n,), nk, dst_fn, dstname)
        else:
            em.op("vector",
                  lambda e: e.tensor_scalar(out=aq_dst, in0=scr1, scalar1=RC, scalar2=-RC,
                                            op0=OP.add, op1=OP.add),
                  reads=(scr1n,), writes=(aq_name,))

    def _attn_phase(self):
        em = self.em
        sm = self.small
        em.phase_barrier()

        em.op("gpsimd", lambda e: e.memset(self.cvt, 0.0), writes=("cvt",))
        em.op("gpsimd", lambda e: e.memset(self.state, 0.0), writes=("state",))
        em.op("gpsimd", lambda e: e.memset(_col(sm, 200), 0.0), writes=("c200",))

        self._load_x(0)
        self._prep_segment(0)
        if self.stop_after == "prep0":
            self._final_drain()
            return
        for s in range(NSEG):
            self._seg_body(s)
            self._chunk(s, 0, do_h14=True)
            if s + 1 < NSEG:
                self._load_x(s + 1)
            self._chunk(s, 1, do_h14=False)
            if s + 1 < NSEG:
                self._prep_segment(s + 1)
            self._h14(s, 1)

        self._ffn_scale_collective()

    def _xtile(self, s, i):
        if i == 0:
            return self.x_sb[:, 0:D], "xs0"
        par = s % 2
        return (self.x_sb[:, D : 2 * D] if par == 0 else self.x_ab[:, 0:D]), f"xs1_{par}"

    def _load_x(self, s):
        t0 = s * SEG
        for i in range(2):
            xt, xn = self._xtile(s, i)
            stream = "xa" if i == 0 else f"xb{s % 2}"
            self.em.dma("sync", stream, xt,
                        self.dx[t0 + i * 128 : t0 + (i + 1) * 128, :],
                        writes=(xn,))

    def _prep_segment(self, s):
        em = self.em
        sm = self.small
        par = s % 2
        cb0 = 50 * par
        scr1 = self.g_sb[:, 0:D]
        scr2 = self.g_sb[:, 1024:2048]
        for i in range(2):
            xi, xname = self._xtile(s, i)
            dst_fn = (lambda i: lambda k:
                      self.aqT[:, k * SEG + i * 128 : k * SEG + i * 128 + 128])(i)
            self._ln_quant_prep(xi, scr1, scr2, "g_sb", "g_sb", xname,
                                None, None, cbase=cb0 + i * 12,
                                dbg=(DEBUG and s == 0 and i == 0),
                                tpose=(8, dst_fn, "aqT"), rtmp=120 + par)
        for i in range(2):
            em.dma("gpsimd", "o",
                   self.d_bn[par : par + 1, 256 + i * 128 : 256 + i * 128 + 128],
                   _col(sm, cb0 + i * 12 + 10), reads=(f"c{cb0 + i * 12 + 10}",),
                   writes=(f"bn_invs{par}",))
        em.dma("sync", "b",
               self.bcinvs.rearrange("p (a f) -> p a f", a=1),
               self.d_bn[par : par + 1, 256 : 256 + SEG]
                   .unsqueeze(0).to_broadcast((128, 1, SEG)),
               reads=(f"bn_invs{par}",), writes=("bc_invs",))

    def _seg_body(self, s):
        em = self.em
        sm = self.small
        for n in range(19):
            nb = min(128, XBCDT - n * 128)
            base = n * (SEG + 4)
            psA = self.ps_a if n % 2 == 0 else self.ps_t
            psAn = "ps_a" if n % 2 == 0 else "ps_t"
            if n < 18:
                em.op("vector",
                      lambda e, base=base, n=n: e.tensor_copy(
                          out=self.xbc[:, base : base + 3], in_=self.cvt[:, n * 3 : n * 3 + 3]),
                      reads=("cvt",), writes=("xbc",))
            for k in range(8):
                em.op("tensor",
                      lambda e, n=n, nb=nb, k=k, psA=psA: e.matmul(
                          psA[0:nb, 0:SEG],
                          self.wq_in[:, k * NPROJ + DIN + n * 128 : k * NPROJ + DIN + n * 128 + nb],
                          self.aqT[:, k * SEG : (k + 1) * SEG],
                          start=(k == 0), stop=(k == 7)),
                      reads=("wq_in", "aqT"), writes=(psAn,))
            em.op("vector",
                  lambda e, base=base, nb=nb, psA=psA: e.tensor_tensor(
                      out=self.xbc[0:nb, base + 3 : base + 3 + SEG],
                      in0=psA[0:nb, 0:SEG], in1=self.bcinvs[0:nb, :], op=OP.mult),
                  reads=(psAn, "bc_invs"), writes=("xbc",))
        if s == 0 and DEBUG:
            em.op("vector", lambda e: e.tensor_copy(out=self.cvs, in_=self.wq_in[:, 2048:2304]),
                  reads=("wq_in",), writes=("cvs",))
            self.tap(self.cvs, ("cvs",))
            self.tap(self.small[:, 0:49], ("c0", "c3", "c10", "c11", "smallc41", "smallc42", "smallc46", "smallc47"))
            self.tap(self.xbc[:, 3 : 3 + 256], ("xbc",))
            self.tap(self.bcinvs, ("bc_invs",))
        for n in range(18):
            base = n * (SEG + 4)
            em.op("vector",
                  lambda e, base=base, n=n: e.tensor_copy(
                      out=self.cvt[:, n * 3 : n * 3 + 3],
                      in_=self.xbc[:, base + SEG : base + SEG + 3]),
                  reads=("xbc",), writes=("cvt",))
            em.op("scalar",
                  lambda e, base=base, n=n: e.activation(
                      out=self.cvs, in_=self.xbc[:, base : base + SEG],
                      func=AF.Identity, scale=_col(self.convw, n * 4),
                      bias=_col(self.convw, 72 + n)),
                  reads=("xbc", "convw"), writes=("cvs",))
            em.op("vector",
                  lambda e, base=base, n=n: e.scalar_tensor_tensor(
                      out=self.cvs, in0=self.xbc[:, base + 1 : base + 1 + SEG],
                      scalar=_col(self.convw, n * 4 + 1), in1=self.cvs,
                      op0=OP.mult, op1=OP.add),
                  reads=("xbc", "convw", "cvs"), writes=("cvs",))
            em.op("vector",
                  lambda e, base=base, n=n: e.scalar_tensor_tensor(
                      out=self.cvs, in0=self.xbc[:, base + 2 : base + 2 + SEG],
                      scalar=_col(self.convw, n * 4 + 2), in1=self.cvs,
                      op0=OP.mult, op1=OP.add),
                  reads=("xbc", "convw", "cvs"), writes=("cvs",))
            em.op("vector",
                  lambda e, base=base, n=n: e.scalar_tensor_tensor(
                      out=self.cvs, in0=self.xbc[:, base + 3 : base + 3 + SEG],
                      scalar=_col(self.convw, n * 4 + 3), in1=self.cvs,
                      op0=OP.mult, op1=OP.add),
                  reads=("xbc", "convw", "cvs"), writes=("cvs",))
            em.op("scalar",
                  lambda e, base=base, n=n: e.activation(
                      out=self.xbc[:, base + 3 : base + 3 + SEG], in_=self.cvs,
                      func=AF.Silu),
                  reads=("cvs",), writes=("xbc",))
        if s == 0:
            self.tap(self.xbc[:, 3 : 3 + 256], ("xbc",))
            self.tap(self.xbc[:, 16 * (SEG + 4) + 3 : 16 * (SEG + 4) + 3 + 256], ("xbc",))
        self._dt_pipeline(s)
        if s == 0:
            self.tap(self.dtp[:, 2 * SEG : 3 * SEG], ("dt2",))
            self.tap(self.dtp[:, 5 * SEG : 6 * SEG], ("dt5",))
            self.tap(self.dtp[:, 6 * SEG : 7 * SEG], ("dt6",))
            self.tap(self.dtp[:, 7 * SEG : 8 * SEG], ("dt7",))

    def _dt_pipeline(self, s):
        em = self.em
        dtp = self.dtp
        SL = SEG

        def sl(i):
            return dtp[:, i * SL : (i + 1) * SL]

        raw = self.xbc[0:H, 18 * (SEG + 4) + 3 : 18 * (SEG + 4) + 3 + SEG]
        dtb = _col(self.consts, 1)[0:H, :]
        negA = _col(self.consts, 2)[0:H, :]
        em.op("scalar",
              lambda e: e.activation(out=sl(0), in_=raw, func=AF.Abs, bias=dtb),
              reads=("xbc", "consts"), writes=("dt0",))
        em.op("vector",
              lambda e: e.tensor_scalar(out=sl(1), in0=raw, scalar1=dtb, scalar2=0.0,
                                        op0=OP.add, op1=OP.max),
              reads=("xbc", "consts"), writes=("dt1",))
        em.op("scalar", lambda e: e.activation(out=sl(0), in_=sl(0), func=AF.Exp, scale=-1.0),
              reads=("dt0",), writes=("dt0",))
        em.op("vector",
              lambda e: e.tensor_scalar(out=sl(2), in0=sl(0), scalar1=-1.0 / 7, scalar2=1.0 / 6,
                                        op0=OP.mult, op1=OP.add),
              reads=("dt0",), writes=("dt2",))
        for kk, c in ((5, 1.0 / 5), (4, 1.0 / 4), (3, 1.0 / 3), (2, 1.0 / 2), (1, 1.0)):
            em.op("vector", lambda e: e.tensor_tensor(out=sl(3), in0=sl(0), in1=sl(2), op=OP.mult),
                  reads=("dt0", "dt2"), writes=("dt3",))
            em.op("vector",
                  lambda e, c=c: e.tensor_scalar(out=sl(2), in0=sl(3), scalar1=-1.0, scalar2=c,
                                                 op0=OP.mult, op1=OP.add),
                  reads=("dt3",), writes=("dt2",))
        em.op("vector", lambda e: e.tensor_tensor(out=sl(2), in0=sl(0), in1=sl(2), op=OP.mult),
              reads=("dt0", "dt2"), writes=("dt2",))
        em.op("vector", lambda e: e.tensor_scalar(out=sl(3), in0=sl(0), scalar1=1.0,
                                                  scalar2=None, op0=OP.add),
              reads=("dt0",), writes=("dt3",))
        em.op("scalar", lambda e: e.activation(out=sl(3), in_=sl(3), func=AF.Ln),
              reads=("dt3",), writes=("dt3",))
        em.op("vector", lambda e: e.tensor_scalar(out=sl(4), in0=sl(0), scalar1=0.368,
                                                  scalar2=None, op0=OP.is_gt),
              reads=("dt0",), writes=("dt4",))
        em.op("vector", lambda e: e.tensor_tensor(out=sl(3), in0=sl(3), in1=sl(2), op=OP.subtract),
              reads=("dt3", "dt2"), writes=("dt3",))
        em.op("vector", lambda e: e.tensor_tensor(out=sl(3), in0=sl(3), in1=sl(4), op=OP.mult),
              reads=("dt3", "dt4"), writes=("dt3",))
        em.op("vector", lambda e: e.tensor_tensor(out=sl(2), in0=sl(2), in1=sl(3), op=OP.add),
              reads=("dt2", "dt3"), writes=("dt2",))
        em.op("vector", lambda e: e.tensor_tensor(out=sl(2), in0=sl(2), in1=sl(1), op=OP.add),
              reads=("dt2", "dt1"), writes=("dt2",))
        em.op("scalar", lambda e: e.activation(out=sl(3), in_=sl(2), func=AF.Ln),
              reads=("dt2",), writes=("dt3",))
        em.op("vector", lambda e: e.tensor_scalar(out=sl(3), in0=sl(3), scalar1=-60000.0,
                                                  scalar2=None, op0=OP.max),
              reads=("dt3",), writes=("dt3",))
        em.op("vector", lambda e: e.tensor_scalar(out=sl(4), in0=sl(2), scalar1=negA,
                                                  scalar2=None, op0=OP.mult),
              reads=("dt2", "consts"), writes=("dt4",))
        for c01 in range(2):
            a_sl = dtp[:, 4 * SL + c01 * CH : 4 * SL + (c01 + 1) * CH]
            acs_sl = dtp[:, 5 * SL + c01 * CH : 5 * SL + (c01 + 1) * CH]
            em.op("vector",
                  lambda e, a_sl=a_sl, acs_sl=acs_sl: e.tensor_tensor_scan(
                      out=acs_sl, data0=a_sl, data1=a_sl, initial=0.0,
                      op0=OP.add, op1=OP.bypass),
                  reads=("dt4",), writes=("dt5",))
            dec_sl = dtp[:, 6 * SL + c01 * CH : 6 * SL + (c01 + 1) * CH]
            em.op("scalar",
                  lambda e, acs_sl=acs_sl, dec_sl=dec_sl, c01=c01: e.activation(
                      out=dec_sl, in_=acs_sl, func=AF.Exp, scale=-1.0,
                      bias=dtp[:, 5 * SL + c01 * CH + CH - 1 : 5 * SL + c01 * CH + CH]),
                  reads=("dt5",), writes=("dt6",))
            em.op("vector",
                  lambda e, dec_sl=dec_sl, c01=c01: e.tensor_tensor(
                      out=dec_sl, in0=dec_sl,
                      in1=dtp[:, 2 * SL + c01 * CH : 2 * SL + (c01 + 1) * CH], op=OP.mult),
                  reads=("dt6", "dt2"), writes=("dt6",))
            cds = dtp[:, 7 * SL + c01 * CH : 7 * SL + (c01 + 1) * CH]
            em.op("vector",
                  lambda e, cds=cds, acs_sl=acs_sl, c01=c01: e.tensor_tensor(
                      out=cds, in0=dtp[:, 3 * SL + c01 * CH : 3 * SL + (c01 + 1) * CH],
                      in1=acs_sl, op=OP.subtract),
                  reads=("dt3", "dt5"), writes=("dt7",))
            em.op("scalar",
                  lambda e, c01=c01: e.activation(
                      out=dtp[:, 8 * SL + c01 : 8 * SL + c01 + 1],
                      in_=dtp[:, 5 * SL + c01 * CH + CH - 1 : 5 * SL + c01 * CH + CH],
                      func=AF.Exp),
                  reads=("dt5",), writes=("dt8",))
        em.dma("gpsimd", "o", self.d_bn[32:64, 0:2],
               self.dtp[:, 8 * SL : 8 * SL + 2], reads=("dt8",), writes=("bn_d",))
        em.dma("sync", "b",
               self.dcol[:, 2 * H : 2 * H + 2 * H].rearrange("p (h c) -> p h c", c=2),
               self.d_bn[32:64, 0:2].unsqueeze(0).to_broadcast((128, H, 2)),
               reads=("bn_d",), writes=("dcol_d",))

    def _chunk(self, s, c01, do_h14=True):
        em = self.em
        sm = self.small
        SL = SEG
        dtp = self.dtp
        PHG = 8
        cb0 = 50 * (s % 2)

        invs_c = _col(sm, cb0 + c01 * 12 + 10)
        for nzg in range(2):
            for k in range(8):
                for j in range(2):
                    nz = nzg * 2 + j
                    pst = self.ps_z if j == 0 else self.ps_o2
                    em.op("tensor",
                          lambda e, k=k, nz=nz, pst=pst: e.matmul(
                              pst[:, 0:512],
                              self.aqT[:, k * SEG + c01 * 128 : k * SEG + c01 * 128 + 128],
                              self.wq_in[:, k * NPROJ + nz * 512 : k * NPROJ + nz * 512 + 512],
                              start=(k == 0), stop=(k == 7)),
                          reads=("aqT", "wq_in"), writes=("ps_z" if j == 0 else "ps_o2",))
            for j in range(2):
                nz = nzg * 2 + j
                pst = self.ps_z if j == 0 else self.ps_o2
                em.op("scalar",
                      lambda e, nz=nz, pst=pst: e.activation(
                          out=self.g_sb[:, nz * 512 : nz * 512 + 512], in_=pst[:, 0:512],
                          func=AF.Silu, scale=invs_c),
                      reads=(("ps_z" if j == 0 else "ps_o2"), f"c{cb0 + c01*12+10}"),
                      writes=("g_sb",))

        cds = dtp[:, 7 * SL + c01 * CH : 7 * SL + (c01 + 1) * CH]
        dec_sl = dtp[:, 6 * SL + c01 * CH : 6 * SL + (c01 + 1) * CH]
        em.op("tensor",
              lambda e: e.matmul(self.ps_c[:, 0:H], cds, self.ident[0:H, 0:H], is_transpose=True),
              reads=("dt7", "ident"), writes=("ps_c",))
        em.op("vector", lambda e: e.tensor_copy(out=self.dcol[:, 0:H], in_=self.ps_c[:, 0:H]),
              reads=("ps_c",), writes=("dcol",))
        em.op("tensor",
              lambda e: e.matmul(self.ps_c[:, 0:H], dec_sl, self.ident[0:H, 0:H], is_transpose=True),
              reads=("dt6", "ident", "dcol"), writes=("ps_c",))
        em.op("vector", lambda e: e.tensor_copy(out=self.dcol[:, H : 2 * H], in_=self.ps_c[:, 0:H]),
              reads=("ps_c",), writes=("dcol",))

        bslice = self.xbc[:, 16 * (SEG + 4) + 3 + c01 * CH : 16 * (SEG + 4) + 3 + (c01 + 1) * CH]
        cslice = self.xbc[:, 17 * (SEG + 4) + 3 + c01 * CH : 17 * (SEG + 4) + 3 + (c01 + 1) * CH]
        em.op("tensor",
              lambda e: e.matmul(self.ps_c[:, 128:256], bslice, cslice),
              reads=("xbc", "dcol"), writes=("ps_c",))
        em.op("vector",
              lambda e: e.tensor_tensor(out=self.cbt, in0=self.ps_c[:, 128:256],
                                        in1=self.mask01, op=OP.mult),
              reads=("ps_c", "mask01"), writes=("cbt",))
        em.op("tensor",
              lambda e: e.matmul(self.ps_c[:, 256:384], bslice, self.ident, is_transpose=True),
              reads=("xbc", "ident", "cbt"), writes=("ps_c",))
        em.op("vector", lambda e: e.tensor_copy(out=self.btok, in_=self.ps_c[:, 256:384]),
              reads=("ps_c",), writes=("btok",))

        acs_sl = dtp[:, 5 * SL + c01 * CH : 5 * SL + (c01 + 1) * CH]
        em.dma("gpsimd", "o", self.d_bn[0:H, 0:CH], acs_sl, reads=("dt5",), writes=("bn_acs",))
        em.op("tensor",
              lambda e: e.matmul(self.ps_y[:, 0:H], acs_sl, self.ident[0:H, 0:H],
                                 is_transpose=True),
              reads=("dt5", "ident"), writes=("ps_y",))
        em.op("scalar",
              lambda e: e.activation(out=sm[:, 148 : 148 + H], in_=self.ps_y[:, 0:H],
                                     func=AF.Exp),
              reads=("ps_y",), writes=("expacsT",))

        def utrans(half):
            h0 = half * 16
            for jg in range(2):
                for j4 in range(4):
                    j = h0 // 2 + jg * 4 + j4
                    src = self.xbc[:, j * (SEG + 4) + 3 + c01 * CH : j * (SEG + 4) + 3 + (c01 + 1) * CH]
                    em.op("tensor",
                          lambda e, j4=j4, src=src: e.matmul(
                              self.ps_t[:, j4 * 128 : j4 * 128 + 128], src, self.ident,
                              is_transpose=True),
                          reads=("xbc", "ident", "u_tok"), writes=("ps_t",))
                em.op("vector",
                      lambda e, jg=jg: e.tensor_copy(
                          out=self.u_tok[:, jg * 512 : jg * 512 + 512], in_=self.ps_t[:, 0:512]),
                      reads=("ps_t",), writes=("u_tok",))

        def stg(gq):
            return self.bcst[:, (gq % 2) * 1024 : (gq % 2) * 1024 + 1024], f"bch{gq % 2}"

        def banks(gq):
            return ((self.ps_y, "ps_y") if gq % 2 == 0 else (self.ps_z, "ps_z"),
                    (self.ps_s, "ps_s") if gq % 2 == 0 else (self.ps_o, "ps_o"),
                    (self.ps_c, "ps_c") if gq % 2 == 0 else (self.ps_t, "ps_t"))

        def schain(gq):
            qh0 = gq * 8
            stage, stgname = stg(gq)
            em.dma("sync", f"b{gq % 2}",
                   stage.rearrange("p (h l) -> p h l", h=PHG),
                   self.d_bn[qh0 : qh0 + PHG, 0:CH].unsqueeze(0).to_broadcast((128, PHG, CH)),
                   reads=("bn_acs",), writes=(stgname,))
            em.op("vector",
                  lambda e, stage=stage, qh0=qh0: e.tensor_tensor(
                      out=stage.rearrange("p (h l) -> p h l", h=PHG),
                      in0=stage.rearrange("p (h l) -> p h l", h=PHG),
                      in1=self.dcol[:, qh0 : qh0 + PHG].unsqueeze(2).broadcast_to(
                          [128, PHG, CH]),
                      op=OP.add),
                  reads=(stgname, "dcol"), writes=(stgname,))
            em.op("vector",
                  lambda e, stage=stage: e.tensor_scalar(
                      out=stage, in0=stage, scalar1=50.0, scalar2=None, op0=OP.min),
                  reads=(stgname,), writes=(stgname,))
            em.op("scalar",
                  lambda e, stage=stage: e.activation(out=stage, in_=stage, func=AF.Exp),
                  reads=(stgname,), writes=(stgname,))
            em.op("vector",
                  lambda e, stage=stage: e.tensor_tensor(
                      out=stage.rearrange("p (h l) -> p h l", h=PHG),
                      in0=stage.rearrange("p (h l) -> p h l", h=PHG),
                      in1=self.cbt.unsqueeze(1).broadcast_to([128, PHG, CH]), op=OP.mult),
                  reads=(stgname, "cbt"), writes=(stgname,))

        def ud_op(gq):
            qh0 = gq * 8
            uoff = (gq % 2) * 512
            em.op("vector",
                  lambda e, uoff=uoff, qh0=qh0: e.tensor_tensor(
                      out=self.uD.rearrange("p (h q) -> p h q", h=PHG),
                      in0=self.u_tok[:, uoff : uoff + 512].rearrange(
                          "p (h q) -> p h q", h=PHG),
                      in1=self.dcol[:, H + qh0 : H + qh0 + PHG].unsqueeze(2).broadcast_to(
                          [128, PHG, PH]),
                      op=OP.mult),
                  reads=("u_tok", "dcol"), writes=("uD",))

        def mms(gq):
            qh0 = gq * 8
            h0 = (gq // 2) * 16
            stage, stgname = stg(gq)
            (psY, psYn), (psS, psSn), (psC, psCn) = banks(gq)
            for hh in range(PHG):
                h = qh0 + hh
                ul = (h - h0) * PH
                em.op("tensor",
                      lambda e, hh=hh, ul=ul, stage=stage, psY=psY: e.matmul(
                          psY[:, hh * PH : (hh + 1) * PH],
                          stage[:, hh * CH : (hh + 1) * CH],
                          self.u_tok[:, ul : ul + PH],
                          start=True, stop=True),
                      reads=(stgname, "u_tok"), writes=(psYn,))
            em.op("tensor",
                  lambda e, qh0=qh0, psC=psC: e.matmul(
                      psC[:, 0:512], cslice,
                      self.state[:, qh0 * PH : qh0 * PH + 512],
                      start=True, stop=True),
                  reads=("xbc", "state"), writes=(psCn,))
            em.op("tensor",
                  lambda e, psS=psS: e.matmul(psS[:, 0:512], self.btok, self.uD,
                                              start=True, stop=True),
                  reads=("btok", "uD"), writes=(psSn,))

        def evacs(gq):
            qh0 = gq * 8
            (psY, psYn), (psS, psSn), (psC, psCn) = banks(gq)
            em.op("vector",
                  lambda e, uoff=(gq % 2) * 512, qh0=qh0: e.tensor_tensor(
                      out=self.y_sb[:, qh0 * PH : qh0 * PH + 512].rearrange(
                          "p (h q) -> p h q", h=PHG),
                      in0=self.u_tok[:, uoff : uoff + 512].rearrange(
                          "p (h q) -> p h q", h=PHG),
                      in1=sm[:, 352 + qh0 : 352 + qh0 + PHG].unsqueeze(2).broadcast_to(
                          [128, PHG, PH]),
                      op=OP.mult),
                  reads=("u_tok", "smallDp"), writes=("y_sb",))
            em.op("vector",
                  lambda e, qh0=qh0, psY=psY: e.tensor_tensor(
                      out=self.y_sb[:, qh0 * PH : qh0 * PH + 512],
                      in0=self.y_sb[:, qh0 * PH : qh0 * PH + 512],
                      in1=psY[:, 0:512], op=OP.add),
                  reads=("y_sb", psYn), writes=("y_sb",))
            em.op("vector",
                  lambda e, qh0=qh0, psC=psC: e.tensor_tensor(
                      out=psC[:, 0:512].rearrange("p (h q) -> p h q", h=PHG),
                      in0=psC[:, 0:512].rearrange("p (h q) -> p h q", h=PHG),
                      in1=sm[:, 148 + qh0 : 148 + qh0 + PHG].unsqueeze(2).broadcast_to(
                          [128, PHG, PH]),
                      op=OP.mult),
                  reads=(psCn, "expacsT"), writes=(psCn,))
            em.op("vector",
                  lambda e, qh0=qh0, psC=psC: e.tensor_tensor(
                      out=self.y_sb[:, qh0 * PH : qh0 * PH + 512],
                      in0=self.y_sb[:, qh0 * PH : qh0 * PH + 512],
                      in1=psC[:, 0:512], op=OP.add),
                  reads=("y_sb", psCn), writes=("y_sb",))
            em.op("vector",
                  lambda e, qh0=qh0: e.tensor_tensor(
                      out=self.y_sb[:, qh0 * PH : qh0 * PH + 512],
                      in0=self.y_sb[:, qh0 * PH : qh0 * PH + 512],
                      in1=self.g_sb[:, qh0 * PH : qh0 * PH + 512], op=OP.mult),
                  reads=("y_sb", "g_sb"), writes=("y_sb",))
            em.op("vector",
                  lambda e, qh0=qh0: e.tensor_tensor(
                      out=self.state[:, qh0 * PH : qh0 * PH + 512].rearrange(
                          "p (h q) -> p h q", h=PHG),
                      in0=self.state[:, qh0 * PH : qh0 * PH + 512].rearrange(
                          "p (h q) -> p h q", h=PHG),
                      in1=self.dcol[:, 2 * H : 4 * H]
                          .rearrange("p (h c) -> p h c", c=2)[:, qh0 : qh0 + PHG,
                                                             c01 : c01 + 1]
                          .broadcast_to([128, PHG, PH]),
                      op=OP.mult),
                  reads=("state", "dcol_d"), writes=("state",))
            em.op("vector",
                  lambda e, qh0=qh0, psS=psS: e.tensor_tensor(
                      out=self.state[:, qh0 * PH : qh0 * PH + 512],
                      in0=self.state[:, qh0 * PH : qh0 * PH + 512],
                      in1=psS[:, 0:512], op=OP.add),
                  reads=("state", psSn), writes=("state",))

        utrans(0)
        schain(0)
        ud_op(0)
        mms(0)
        schain(1)
        evacs(0)
        ud_op(1)
        mms(1)
        schain(2)
        evacs(1)
        utrans(1)
        ud_op(2)
        mms(2)
        schain(3)
        evacs(2)
        ud_op(3)
        mms(3)
        evacs(3)

        if s == 0 and c01 == 0:
            self.tap(self.g_sb[:, 0:512], ("g_sb",))
            self.tap(self.u_tok[:, 512:1024], ("u_tok",))
            self.tap(self.cbt, ("cbt",))
            self.tap(self.y_sb[:, 0:512], ("y_sb",))
            self.tap(self.state[:, 0:512], ("state",))
            self.tap(self.dcol[:, 0:128], ("dcol",))
        self._out_prep(c01)
        if DEBUG and s == 0 and c01 == 0:
            self.tap(self.bcst[:, 0:512], ("bch0", "bch1"))
            for _t in range(4):
                self.tap(self.y_sb[:, _t * 512 : (_t + 1) * 512], ("y_sb",))
            em.op("vector", lambda e: e.tensor_copy(out=self.uD, in_=self.wq_out[:, 0:512]),
                  reads=("wq_out",), writes=("uD",))
            self.tap(self.uD, ("uD",))
        for no in range(2):
            pst = self.ps_o if no == 0 else self.ps_o2
            pname = "ps_o" if no == 0 else "ps_o2"
            for k in range(16):
                em.op("tensor",
                      lambda e, k=k, no=no, pst=pst: e.matmul(
                          pst[:, 0:512], self.yqT[:, k * CH : (k + 1) * CH],
                          self.wq_out[:, k * D + no * 512 : k * D + no * 512 + 512],
                          start=(k == 0), stop=(k == 15)),
                      reads=("yqT", "wq_out"), writes=(pname,))
        if do_h14:
            self._h14(s, c01)

    def _h14(self, s, c01):
        em = self.em
        sm = self.small
        xm, xname = self._xtile(s, c01)
        invs2 = _col(sm, 37)
        cb1 = 50 * (s % 2) + c01 * 12
        em.op("vector",
              lambda e: e.tensor_scalar(out=_col(sm, 215), in0=_col(sm, cb1 + 3),
                                        scalar1=1.0, scalar2=None, op0=OP.add),
              reads=(f"c{cb1+3}",), writes=("c215",))
        em.op("vector",
              lambda e: e.scalar_tensor_tensor(out=_col(sm, 216), in0=_col(sm, cb1), scalar=-1.0,
                                               in1=_col(sm, cb1 + 3), op0=OP.mult, op1=OP.mult),
              reads=(f"c{cb1}", f"c{cb1+3}"), writes=("c216",))
        em.op("scalar",
              lambda e: e.activation(out=xm, in_=xm,
                                     func=AF.Identity, scale=_col(sm, 215), bias=_col(sm, 216)),
              reads=(xname, "c215", "c216"), writes=(xname,))
        for no in range(2):
            pst = self.ps_o if no == 0 else self.ps_o2
            pname = "ps_o" if no == 0 else "ps_o2"
            em.op("vector",
                  lambda e, no=no, pst=pst: e.scalar_tensor_tensor(
                      out=xm[:, no * 512 : no * 512 + 512],
                      in0=pst[:, 0:512], scalar=invs2,
                      in1=xm[:, no * 512 : no * 512 + 512],
                      op0=OP.mult, op1=OP.add,
                      accum_out=_col(sm, 204 + no)),
                  reads=(pname, "c37", xname), writes=(xname, f"c{204+no}"))
        mti = 2 * s + c01
        c = lambda i: _col(sm, 206 + i)
        em.op("vector",
              lambda e: e.tensor_tensor(out=c(0), in0=_col(sm, 204), in1=_col(sm, 205), op=OP.add),
              reads=("c204", "c205"), writes=("c206",))
        em.op("scalar",
              lambda e: e.activation(out=self.u_tok[:, 0:D], in_=xm, func=AF.Square,
                                     accum_out=c(1)),
              reads=(xname,), writes=("u_tok", "c207"))
        em.op("vector", lambda e: e.tensor_scalar(out=c(0), in0=c(0), scalar1=1.0 / D,
                                                  scalar2=None, op0=OP.mult),
              reads=("c206",), writes=("c206",))
        em.op("scalar", lambda e: e.activation(out=c(2), in_=c(0), func=AF.Square),
              reads=("c206",), writes=("c208",))
        em.op("vector",
              lambda e: e.scalar_tensor_tensor(out=c(1), in0=c(1), scalar=1.0 / D, in1=c(2),
                                               op0=OP.mult, op1=OP.subtract),
              reads=("c207", "c208"), writes=("c207",))
        self._rsqrt_col(c(3), c(1), EPS_LN, ("c207",), "c209", tmp=123)
        em.op("vector",
              lambda e: e.tensor_copy(out=_col(self.ffcol, 2 * mti + 1), in_=c(3)),
              reads=("c209",), writes=("ffcol",))
        em.op("vector",
              lambda e: e.tensor_tensor(out=_col(self.ffcol, 2 * mti), in0=c(0), in1=c(3),
                                        op=OP.mult),
              reads=("c206", "c209"), writes=("ffcol",))
        em.op("vector",
              lambda e: e.scalar_tensor_tensor(out=c(2), in0=c(0), scalar=-1.0, in1=c(3),
                                               op0=OP.mult, op1=OP.mult),
              reads=("c206", "c209"), writes=("c208",))
        em.op("scalar",
              lambda e: e.activation(out=self.u_tok[:, 0:D], in_=xm, func=AF.Abs,
                                     scale=c(3), bias=c(2)),
              reads=(xname, "c209", "c208"), writes=("u_tok",))
        em.op("vector",
              lambda e: e.tensor_reduce(out=c(4), in_=self.u_tok[:, 0:D],
                                        axis=mybir.AxisListType.X, op=OP.max),
              reads=("u_tok",), writes=("c210",))
        em.op("vector",
              lambda e: e.tensor_tensor(out=_col(sm, 200), in0=_col(sm, 200), in1=c(4), op=OP.max),
              reads=("c200", "c210"), writes=("c200",))
        em.dma("gpsimd", "o", self.d_xmid[(2 * s + c01) * 128 : (2 * s + c01 + 1) * 128, :],
               xm, reads=(xname,))

    def _out_prep(self, c01):
        em = self.em
        sm = self.small
        cb = 26
        c = lambda i: _col(sm, cb + i)
        cn = lambda i: f"c{cb + i}"
        y = self.y_sb
        scr = self.g_sb
        scrn = "g_sb"
        scr2 = self.bcst[:, 0:2048]
        scr2n = ("bch0", "bch1")
        em.op("scalar",
              lambda e: e.activation(out=scr, in_=y, func=AF.Square, accum_out=c(0)),
              reads=("y_sb",), writes=(scrn, cn(0)))
        em.op("vector", lambda e: e.tensor_scalar(out=c(0), in0=c(0), scalar1=1.0 / DIN,
                                                  scalar2=None, op0=OP.mult),
              reads=(cn(0),), writes=(cn(0),))
        self._rsqrt_col(c(1), c(0), EPS_RMS, (cn(0),), cn(1), tmp=122)
        em.op("vector",
              lambda e: e.scalar_tensor_tensor(out=scr, in0=y, scalar=c(1), in1=self.bc_w2,
                                               op0=OP.mult, op1=OP.mult, accum_out=c(2)),
              reads=("y_sb", cn(1), "bc_w2"), writes=(scrn, cn(2)))
        em.op("scalar",
              lambda e: e.activation(out=scr2, in_=scr, func=AF.Square, accum_out=c(3)),
              reads=(scrn,), writes=scr2n + (cn(3),))
        em.op("vector", lambda e: e.tensor_scalar(out=c(2), in0=c(2), scalar1=1.0 / DIN,
                                                  scalar2=None, op0=OP.mult),
              reads=(cn(2),), writes=(cn(2),))
        em.op("scalar", lambda e: e.activation(out=c(4), in_=c(2), func=AF.Square),
              reads=(cn(2),), writes=(cn(4),))
        em.op("vector",
              lambda e: e.scalar_tensor_tensor(out=c(3), in0=c(3), scalar=1.0 / DIN, in1=c(4),
                                               op0=OP.mult, op1=OP.subtract),
              reads=(cn(3), cn(4)), writes=(cn(3),))
        self._rsqrt_col(c(5), c(3), EPS_LN, (cn(3),), cn(5), tmp=122)
        em.op("vector",
              lambda e: e.tensor_reduce(out=c(4), in_=scr, axis=mybir.AxisListType.X, op=OP.max),
              reads=(scrn,), writes=(cn(4),))
        em.op("vector",
              lambda e: e.tensor_reduce(out=c(6), in_=scr, axis=mybir.AxisListType.X, op=OP.min),
              reads=(scrn,), writes=(cn(6),))
        em.op("vector", lambda e: e.tensor_tensor(out=c(4), in0=c(4), in1=c(2), op=OP.subtract),
              reads=(cn(4), cn(2)), writes=(cn(4),))
        em.op("vector", lambda e: e.tensor_tensor(out=c(6), in0=c(2), in1=c(6), op=OP.subtract),
              reads=(cn(6), cn(2)), writes=(cn(6),))
        em.op("vector", lambda e: e.tensor_tensor(out=c(4), in0=c(4), in1=c(6), op=OP.max),
              reads=(cn(4), cn(6)), writes=(cn(4),))
        em.op("vector", lambda e: e.tensor_tensor(out=c(4), in0=c(4), in1=c(5), op=OP.mult),
              reads=(cn(4), cn(5)), writes=(cn(4),))
        em.op("vector", lambda e: e.tensor_scalar(out=c(4), in0=c(4), scalar1=1e-5,
                                                  scalar2=None, op0=OP.max),
              reads=(cn(4),), writes=(cn(4),))
        em.op("vector", lambda e: e.tensor_scalar(out=c(10), in0=c(4), scalar1=1.0 / 127.0,
                                                  scalar2=None, op0=OP.mult),
              reads=(cn(4),), writes=(cn(10),))
        em.op("vector", lambda e: e.reciprocal(out=c(11), in_=c(10)),
              reads=(cn(10),), writes=(cn(11),))
        em.op("vector", lambda e: e.tensor_tensor(out=c(5), in0=c(5), in1=c(11), op=OP.mult),
              reads=(cn(5), cn(11)), writes=(cn(5),))
        em.op("vector",
              lambda e: e.scalar_tensor_tensor(out=c(6), in0=c(2), scalar=-1.0, in1=c(5),
                                               op0=OP.mult, op1=OP.mult),
              reads=(cn(2), cn(5)), writes=(cn(6),))
        em.op("scalar",
              lambda e: e.activation(out=scr2, in_=scr, func=AF.Identity, scale=c(5), bias=c(6)),
              reads=(scrn, cn(5), cn(6)), writes=scr2n)
        self._pe_round_T(scr2, scr2n, 16,
                         lambda k: self.yqT[:, k * CH : (k + 1) * CH], "yqT")
        em.op("vector", lambda e: e.tensor_copy(out=_col(sm, 37), in_=c(10)),
              reads=(cn(10),), writes=("c37",))

    def _ffn_scale_collective(self):
        em = self.em
        sm = self.small
        w1v = self.wq_in[:, 0 : 8 * DFF]
        em.dma("sync", "wl0", w1v[:, 0 : 4 * DFF].rearrange("p (k n) -> p k n", k=4),
               self.d_w1q.rearrange("(k p) n -> p k n", p=128)[:, 0:4],
               writes=("w1va", "wq_in"))
        em.dma("sync", "wl1", w1v[:, 4 * DFF : 8 * DFF].rearrange("p (k n) -> p k n", k=4),
               self.d_w1q.rearrange("(k p) n -> p k n", p=128)[:, 4:8],
               writes=("w1vb", "wq_in"))
        self._fold_partitions(_col(sm, 200), "c200", OP.max, 201, "c201")
        em.dma("gpsimd", "o", self.d_cc1i, sm[0:1, 201:202], reads=("c201",), writes=("cc1in",))
        em.op("gpsimd",
              lambda e: e.collective_compute(
                  "AllReduce", OP.max, replica_groups=[list(range(8))],
                  ins=[self.d_cc1i.opt()], outs=[self.d_cc1o.opt()]),
              reads=("cc1in",), writes=("cc1",))
        em.dma("sync", "b", _col(self.ffs, 0).rearrange("p (a f) -> p a f", a=1),
               self.d_cc1o.unsqueeze(0).to_broadcast((128, 1, 1)),
               reads=("cc1",), writes=("ffs0",))
        em.op("vector",
              lambda e: e.tensor_scalar(out=_col(self.ffs, 1), in0=_col(self.ffs, 0),
                                        scalar1=1e-6, scalar2=1.0 / 127.0,
                                        op0=OP.max, op1=OP.mult),
              reads=("ffs0",), writes=("ffs1",))
        em.op("vector", lambda e: e.reciprocal(out=_col(self.ffs, 2), in_=_col(self.ffs, 1)),
              reads=("ffs1",), writes=("ffs2",))
        em.op("vector",
              lambda e: e.tensor_tensor(out=_col(self.ffs, 3), in0=_col(self.ffs, 1),
                                        in1=_col(sm, 42), op=OP.mult),
              reads=("ffs1", "smallc42"), writes=("ffs3",))
        em.op("vector",
              lambda e: e.tensor_scalar(out=sm[:, 320 : 320 + 32], in0=sm[:, 320 : 320 + 32],
                                        scalar1=_col(self.ffs, 3), scalar2=None, op0=OP.mult),
              reads=("smallB1", "ffs3"), writes=("smallB1",))

    def _ffn_phase(self):
        em = self.em
        nc = self.nc
        sm = self.small
        em.phase_barrier()
        w1v = self.wq_in[:, 0 : 8 * DFF]
        em.op("gpsimd", lambda e: e.memset(_col(sm, 223), 0.0), writes=("c223",))

        def xload_f(fs):
            p = fs % 2
            dst = self.x4 if p == 0 else self.xbc[:, 0:4096]
            em.dma("sync", f"fx{p}",
                   dst.rearrange("p (i d) -> p i d", i=4),
                   self.d_xmid[fs * SEG2 : (fs + 1) * SEG2, :].rearrange(
                       "(i p) d -> p i d", p=128),
                   writes=(f"xf{p}",))

        xload_f(0)
        for fs in range(T // SEG2):
            t0 = fs * SEG2
            if fs + 1 < T // SEG2:
                xload_f(fs + 1)
            xbuf = self.x4 if fs % 2 == 0 else self.xbc[:, 0:4096]
            xbn = f"xf{fs % 2}"
            for i in range(4):
                mti = fs * 4 + i
                xi = xbuf[:, i * D : (i + 1) * D]
                scr = self.u_tok
                em.op("vector",
                      lambda e, mti=mti: e.tensor_scalar(
                          out=_col(sm, 220), in0=_col(self.ffcol, 2 * mti + 1),
                          scalar1=_col(self.ffs, 2), scalar2=None, op0=OP.mult),
                      reads=("ffcol", "ffs2"), writes=("c220",))
                em.op("vector",
                      lambda e, mti=mti: e.tensor_scalar(
                          out=_col(sm, 221), in0=_col(self.ffcol, 2 * mti),
                          scalar1=_col(self.ffs, 2), scalar2=-1.0,
                          op0=OP.mult, op1=OP.mult),
                      reads=("ffcol", "ffs2"), writes=("c221",))

                em.op("scalar",
                      lambda e, xi=xi, scr=scr: e.activation(out=scr[:, 0:D], in_=xi, func=AF.Identity,
                                                    scale=_col(sm, 220), bias=_col(sm, 221)),
                      reads=(xbn, "c220", "c221"), writes=("u_tok",))
                self._pe_round_T(
                    scr[:, 0:D], ("u_tok",), 8,
                    (lambda i: lambda k:
                     self.aq1T[:, k * SEG2 + i * 128 : k * SEG2 + i * 128 + 128])(i),
                    "aq1T")
            for n in range(DFF // 128):
                pst = self.ps_o if n % 2 == 0 else self.ps_o2
                pname = "ps_o" if n % 2 == 0 else "ps_o2"
                for k in range(8):
                    em.op("tensor",
                          lambda e, n=n, k=k, pst=pst: e.matmul(
                              pst[:, 0:512],
                              w1v[:, k * DFF + n * 128 : k * DFF + n * 128 + 128],
                              self.aq1T[:, k * SEG2 : (k + 1) * SEG2],
                              start=(k == 0), stop=(k == 7)),
                          reads=("w1va", "w1vb", "aq1T"), writes=(pname,))
                hn = f"hsb{n % 2}"
                hsb = self.y_sb if n % 2 == 0 else self.state
                em.op("scalar",
                      lambda e, n=n, pst=pst, hsb=hsb: e.activation(
                          out=hsb[:, 0:512], in_=pst[:, 0:512], func=AF.Gelu,
                          scale=_col(self.ffs, 3), bias=_col(sm, 320 + n)),
                      reads=(pname, "ffs3", "smallB1"), writes=(hn,))
                em.op("vector",
                      lambda e, hsb=hsb: e.tensor_reduce(
                          out=_col(sm, 222), in_=hsb[:, 0:512],
                          axis=mybir.AxisListType.X, op=OP.max, apply_absolute_value=True),
                      reads=(hn,), writes=("c222",))
                em.op("vector",
                      lambda e: e.tensor_tensor(out=_col(sm, 223), in0=_col(sm, 223),
                                                in1=_col(sm, 222), op=OP.max),
                      reads=("c223", "c222"), writes=("c223",))
                em.dma("gpsimd", "h", self.d_h[n * 128 : (n + 1) * 128, t0 : t0 + SEG2],
                       hsb[:, 0:512], reads=(hn,), writes=("d_hbuf",))

        w2v = self.wq_in[:, 0 : 32 * D]
        em.dma("sync", "wl0", w2v[:, 0 : 16 * D].rearrange("p (k n) -> p k n", k=16),
               self.d_w2q.rearrange("(k p) n -> p k n", p=128)[:, 0:16], writes=("w1va",))
        em.dma("sync", "wl1", w2v[:, 16 * D : 32 * D].rearrange("p (k n) -> p k n", k=16),
               self.d_w2q.rearrange("(k p) n -> p k n", p=128)[:, 16:32], writes=("w1vb",))
        em.dma("sync", "b", self.bc_w.rearrange("p (a f) -> p a f", a=1),
               self.d_b2.unsqueeze(0).to_broadcast((128, 1, D)), writes=("bc_w",))
        self._fold_partitions(_col(sm, 223), "c223", OP.max, 224, "c224")
        em.dma("gpsimd", "o", self.d_cc2i, sm[0:1, 224:225], reads=("c224",), writes=("cc2in",))
        em.op("gpsimd",
              lambda e: e.collective_compute(
                  "AllReduce", OP.max, replica_groups=[list(range(8))],
                  ins=[self.d_cc2i.opt()], outs=[self.d_cc2o.opt()]),
              reads=("cc2in",), writes=("cc2",))
        em.dma("sync", "b", _col(self.ffs, 4).rearrange("p (a f) -> p a f", a=1),
               self.d_cc2o.unsqueeze(0).to_broadcast((128, 1, 1)),
               reads=("cc2",), writes=("ffs4",))
        em.op("vector",
              lambda e: e.tensor_scalar(out=_col(self.ffs, 5), in0=_col(self.ffs, 4),
                                        scalar1=1e-6, scalar2=1.0 / 127.0,
                                        op0=OP.max, op1=OP.mult),
              reads=("ffs4",), writes=("ffs5",))
        em.op("vector", lambda e: e.reciprocal(out=_col(self.ffs, 6), in_=_col(self.ffs, 5)),
              reads=("ffs5",), writes=("ffs6",))
        em.op("vector",
              lambda e: e.tensor_tensor(out=_col(self.ffs, 7), in0=_col(self.ffs, 5),
                                        in1=_col(sm, 45), op=OP.mult),
              reads=("ffs5", "smallc45"), writes=("ffs7",))

        em.op("vector",
              lambda e: e.tensor_scalar(out=self.bc_w, in0=self.bc_w,
                                        scalar1=_col(self.ffs, 7), scalar2=None, op0=OP.mult),
              reads=("bc_w", "ffs7"), writes=("bc_w",))

        for fs in range(T // SEG2):
            t0 = fs * SEG2
            em.dma("sync", "x",
                   self.x4.rearrange("p (i d) -> p i d", i=4),
                   self.d_xmid[t0 : t0 + SEG2, :].rearrange("(i p) d -> p i d", p=128),
                   writes=("xseg",))
            for i in range(4):
                hraw = self.bcst
                for hf in range(2):
                    em.dma("sync", "h2",
                           hraw.rearrange("p (k t) -> p k t", k=16),
                           self.d_h.rearrange("(k p) t -> p k t", p=128)[
                               :, hf * 16 : (hf + 1) * 16, t0 + i * 128 : t0 + (i + 1) * 128],
                           reads=("d_hbuf",), writes=("bch0", "bch1"))
                    em.op("scalar",
                          lambda e: e.activation(
                              out=self.y_sb, in_=hraw,
                              func=AF.Identity, scale=_col(self.ffs, 6), bias=_col(self.consts, 3)),
                          reads=("bch0", "bch1", "ffs6", "rc_col"), writes=("hsb0",))
                    em.op("vector",
                          lambda e, hf=hf: e.tensor_scalar(
                              out=self.hq[:, hf * 2048 : (hf + 1) * 2048], in0=self.y_sb,
                              scalar1=-RC, scalar2=None, op0=OP.add),
                          reads=("hsb0",), writes=("hq",))
                for no in range(2):
                    pst = self.ps_o if no == 0 else self.ps_o2
                    pname = "ps_o" if no == 0 else "ps_o2"
                    for k in range(32):
                        em.op("tensor",
                              lambda e, k=k, no=no, pst=pst: e.matmul(
                                  pst[:, 0:512], self.hq[:, k * 128 : (k + 1) * 128],
                                  w2v[:, k * D + no * 512 : k * D + no * 512 + 512],
                                  start=(k == 0), stop=(k == 31)),
                              reads=("hq", "w1va", "w1vb"), writes=(pname,))
                xi = self.x4[:, i * D : (i + 1) * D]
                em.op("vector",
                      lambda e, xi=xi: e.tensor_tensor(out=xi, in0=xi, in1=self.bc_w, op=OP.add),
                      reads=("xseg", "bc_w"), writes=("xseg",))
                for no in range(2):
                    pst = self.ps_o if no == 0 else self.ps_o2
                    pname = "ps_o" if no == 0 else "ps_o2"
                    em.op("vector",
                          lambda e, xi=xi, no=no, pst=pst: e.scalar_tensor_tensor(
                              out=xi[:, no * 512 : no * 512 + 512], in0=pst[:, 0:512],
                              scalar=_col(self.ffs, 7), in1=xi[:, no * 512 : no * 512 + 512],
                              op0=OP.mult, op1=OP.add),
                          reads=(pname, "ffs7", "xseg"), writes=("xseg",))
                em.dma("gpsimd", "o", self.d_out[t0 + i * 128 : t0 + (i + 1) * 128, :],
                       xi, reads=("xseg",))

    def _assemble(self):
        nc = self.nc
        em = self.em
        with self.stack:
            em.finalize_sems(self.stack)
            with nc.Block() as blk:

                @blk.tensor
                def _(e):
                    em.replay("tensor", e)

                @blk.vector
                def _(e):
                    em.replay("vector", e)

                @blk.scalar
                def _(e):
                    em.replay("scalar", e)

                @blk.gpsimd
                def _(e):
                    em.replay("gpsimd", e)

                @blk.sync
                def _(e):
                    em.replay("sync", e)


_NC_CACHE = {}


def _build(stop_after=None):
    key = stop_after
    if key not in _NC_CACHE:
        nc = bass.Bass("TRN2", target_bir_lowering=False)
        Build(nc, stop_after=stop_after)
        _NC_CACHE[key] = nc
    return _NC_CACHE[key]


def make_in_map(inputs, core):
    x = np.ascontiguousarray(inputs["L0_attn_hidden_states"][core]).astype(np.float32)
    m = {
        "x": x,
        "norm_w": np.ascontiguousarray(inputs["L0_attn_norm_w"]).reshape(1, D),
        "win_t": np.ascontiguousarray(np.asarray(inputs["L0_attn_in_proj_w"]).T),
        "conv_w": np.ascontiguousarray(inputs["L0_attn_conv_w"]),
        "conv_b": np.ascontiguousarray(inputs["L0_attn_conv_b"]).reshape(CONVD, 1),
        "a_log": np.ascontiguousarray(inputs["L0_attn_A_log"]).reshape(H, 1),
        "dp": np.ascontiguousarray(inputs["L0_attn_Dp"]).reshape(1, H),
        "dt_bias": np.ascontiguousarray(inputs["L0_attn_dt_bias"]).reshape(H, 1),
        "out_norm_w": np.ascontiguousarray(inputs["L0_attn_out_norm_w"]).reshape(1, DIN),
        "wout_t": np.ascontiguousarray(np.asarray(inputs["L0_attn_out_proj_w"]).T),
        "w1_t": np.ascontiguousarray(np.asarray(inputs["L0_ffn_w1"]).T),
        "b1": np.ascontiguousarray(inputs["L0_ffn_b1"]).reshape(DFF, 1),
        "w2_t": np.ascontiguousarray(np.asarray(inputs["L0_ffn_w2"]).T),
        "b2": np.ascontiguousarray(inputs["L0_ffn_b2"]).reshape(1, D),
    }
    return {k: np.ascontiguousarray(np.asarray(v, dtype=np.float32)) for k, v in m.items()}


def kernel(**inputs):
    nc = _build()
    in_maps = [make_in_map(inputs, c) for c in range(8)]
    res = run_bass_kernel_spmd(nc, in_maps, core_ids=list(range(8)))
    out = np.stack([np.asarray(r["out"]) for r in res.results], axis=0)
    return out.astype(np.float32)


# version 27
# speedup vs baseline: 1.2093x; 1.2093x over previous
import contextlib

import numpy as np

import concourse.bass as bass
import concourse.mybir as mybir
from concourse.alu_op_type import AluOpType as OP
from concourse.bass_utils import run_bass_kernel_spmd

F32 = mybir.dt.float32
BF16 = mybir.dt.bfloat16
AF = mybir.ActivationFunctionType

RC = 12582912.0
DEBUG = 0

T = 2048
D = 1024
DIN = 2048
NST = 128
H = 32
PH = 64
CONVD = 2304
XBCDT = 2336
NPROJ = 4384
DFF = 4096
SEG = 256
NSEG = T // SEG
CH = 128
SEG2 = 512
EPS_LN = 1e-5
EPS_RMS = 1e-6

ENGS = ("tensor", "vector", "scalar", "gpsimd", "sync")


class Emitter:

    def __init__(self, nc):
        self.nc = nc
        self.recs = {e: [] for e in ENGS}
        self.cnt = {e: 0 for e in ENGS}
        self.dcnt = {}
        self.dma_recs = {}
        self.buf = {}
        self.watermark = {e: {} for e in ENGS}
        self.sem_handles = {}
        self.targets = {}

    def _wait(self, eng, prod_key, val):
        if val <= 0:
            return
        wm = self.watermark[eng]
        if wm.get(prod_key, 0) >= val:
            return
        wm[prod_key] = val
        self.targets.setdefault(prod_key, set()).add(val)
        self.recs[eng].append({"k": "wait", "pk": prod_key, "v": val})

    def _pre(self, eng, reads, writes):
        me = ("e", eng)
        for b in reads:
            st = self.buf.get(b)
            if st and st["w"]:
                if st["w"][0] != me:
                    self._wait(eng, st["w"][0], st["w"][1])
                elif self.cnt[eng] - st["w"][1] <= 2:
                    self._wait(eng, st["w"][0], st["w"][1])
        for b in writes:
            st = self.buf.get(b)
            if st:
                if st["w"]:
                    if st["w"][0] != me:
                        self._wait(eng, st["w"][0], st["w"][1])
                    elif self.cnt[eng] - st["w"][1] <= 2:
                        self._wait(eng, st["w"][0], st["w"][1])
                for rk, rv in st["r"].items():
                    if rk != me:
                        self._wait(eng, rk, rv)

    def _post(self, kind, key, count, reads, writes):
        k = (kind, key)
        for b in writes:
            self.buf[b] = {"w": (k, count), "r": {}}
        for b in reads:
            st = self.buf.setdefault(b, {"w": None, "r": {}})
            st["r"][k[0:2]] = count

    def op(self, eng, fn, reads=(), writes=()):
        self._pre(eng, reads, writes)
        self.cnt[eng] += 1
        self.recs[eng].append({"k": "op", "fn": fn, "c": self.cnt[eng]})
        self._post("e", eng, self.cnt[eng], reads, writes)

    def dma(self, eng, stream, out, in_, reads=(), writes=(), transpose=False,
            slow=False):
        self._pre(eng, reads, writes)
        self.dcnt[stream] = self.dcnt.get(stream, 0) + 16
        self.recs[eng].append({"k": "dma", "out": out, "in": in_, "t": transpose,
                               "s": slow, "stream": stream, "c": self.dcnt[stream]})
        self._post("d", "dma_" + stream, self.dcnt[stream], reads, writes)

    def raw(self, eng, fn):
        self.recs[eng].append({"k": "raw", "fn": fn})

    def phase_barrier(self):
        for eng in ENGS:
            for e2 in ENGS:
                if e2 != eng and self.cnt[e2] > 0:
                    self._wait(eng, ("e", e2), self.cnt[e2])
            for s, v in self.dcnt.items():
                self._wait(eng, ("d", "dma_" + s), v)

    def finalize_sems(self, stack):
        for e in ENGS:
            self.sem_handles[e] = stack.enter_context(self.nc.semaphore(f"s_{e}"))
        for s in self.dcnt:
            self.sem_handles["dma_" + s] = stack.enter_context(
                self.nc.semaphore(f"sd_{s}")
            )
        self.cmap = {}
        for pk, vals in self.targets.items():
            self.cmap[pk] = sorted(vals)

    def _remap(self, pk, v):
        import bisect
        if pk[0] == "d":
            return v
        lst = self.cmap[pk]
        return bisect.bisect_left(lst, v) + 1

    def replay(self, eng, e):
        import bisect
        for r in self.recs[eng]:
            k = r["k"]
            if k == "wait":
                pk = r["pk"]
                sem = self.sem_handles[pk[1]]
                e.wait_ge(sem, self._remap(pk, r["v"]))
            elif k == "op":
                inst = r["fn"](e)
                pk = ("e", eng)
                lst = self.cmap.get(pk)
                if lst and r["c"] in self.targets.get(pk, ()):
                    inst.then_inc(self.sem_handles[eng], 1)
            elif k == "dma":
                if r["t"]:
                    inst = e.dma_start_transpose(out=r["out"], in_=r["in"])
                elif r["s"]:
                    inst = e.dma_start(out=r["out"], in_=r["in"],
                                       allow_slow_non_contiguous=True)
                else:
                    inst = e.dma_start(out=r["out"], in_=r["in"])
                inst.then_inc(self.sem_handles["dma_" + r["stream"]], 16)
            else:
                r["fn"](e)


def _col(t, i):
    return t[:, i : i + 1]


class Build:
    def __init__(self, nc, stop_after=None):
        self.nc = nc
        self.stop_after = stop_after
        self.stack = contextlib.ExitStack()
        self.em = Emitter(nc)
        self._declare_dram()
        self._declare_sbuf()
        self._emit_program()
        self._assemble()

    def _declare_dram(self):
        nc = self.nc
        self.dx = nc.dram_tensor("x", [T, D], F32, kind="ExternalInput").ap()
        self.d_norm_w = nc.dram_tensor("norm_w", [1, D], F32, kind="ExternalInput").ap()
        self.d_win = nc.dram_tensor("win_t", [D, NPROJ], F32, kind="ExternalInput").ap()
        self.d_conv_w = nc.dram_tensor("conv_w", [CONVD, 4], F32, kind="ExternalInput").ap()
        self.d_conv_b = nc.dram_tensor("conv_b", [CONVD, 1], F32, kind="ExternalInput").ap()
        self.d_alog = nc.dram_tensor("a_log", [H, 1], F32, kind="ExternalInput").ap()
        self.d_dp = nc.dram_tensor("dp", [1, H], F32, kind="ExternalInput").ap()
        self.d_dtb = nc.dram_tensor("dt_bias", [H, 1], F32, kind="ExternalInput").ap()
        self.d_onw = nc.dram_tensor("out_norm_w", [1, DIN], F32, kind="ExternalInput").ap()
        self.d_wout = nc.dram_tensor("wout_t", [DIN, D], F32, kind="ExternalInput").ap()
        self.d_w1 = nc.dram_tensor("w1_t", [D, DFF], F32, kind="ExternalInput").ap()
        self.d_b1 = nc.dram_tensor("b1", [DFF, 1], F32, kind="ExternalInput").ap()
        self.d_w2 = nc.dram_tensor("w2_t", [DFF, D], F32, kind="ExternalInput").ap()
        self.d_b2 = nc.dram_tensor("b2", [1, D], F32, kind="ExternalInput").ap()
        self.d_out = nc.dram_tensor("out", [T, D], F32, kind="ExternalOutput").ap()

        ik = "ExternalOutput" if DEBUG else "Internal"
        self.d_xmid = nc.dram_tensor("xmid", [T, D], F32, kind=ik).ap()
        self.d_h = nc.dram_tensor("hbuf", [DFF, T], F32, kind=ik).ap()
        self.d_w1q = nc.dram_tensor("w1q_spill", [D, DFF], BF16).ap()
        self.d_w2q = nc.dram_tensor("w2q_spill", [DFF, D], BF16).ap()
        self.d_bn = nc.dram_tensor("bounce", [64, 512], F32).ap()
        self.d_cc1i = nc.dram_tensor("cc1i", [1, 1], F32).ap()
        self.d_cc1o = nc.dram_tensor("cc1o", [1, 1], F32).ap()
        self.d_cc2i = nc.dram_tensor("cc2i", [1, 1], F32).ap()
        self.d_cc2o = nc.dram_tensor("cc2o", [1, 1], F32).ap()
        self.ntap = 0
        if DEBUG:
            self.d_dbg = nc.dram_tensor("dbg", [128, 64 * 512], F32, kind="ExternalOutput").ap()

    def _sb(self, name, p, f, dt=F32):
        return self.stack.enter_context(self.nc.sbuf_tensor(name, [p, f], dt))

    def _declare_sbuf(self):
        sb = self._sb
        self.wq_in = sb("wq_in", 128, 8 * NPROJ, BF16)
        self.wq_out = sb("wq_out", 128, 16 * D, BF16)
        self.arf = sb("arf", 128, 22720)
        a = self.arf
        o = 0

        def take(n):
            nonlocal o
            v = a[:, o : o + n]
            o += n
            return v

        x_off = o
        self.x_sb = take(2 * D)
        self.g_sb = take(DIN)
        self.x4 = self.arf[:, x_off : x_off + 4 * D]
        self.xbc = take(19 * (SEG + 4))
        self.u_tok = take(DIN // 2)
        self.bcst = take(2048)
        self.y_sb = take(DIN)
        self.state = take(DIN)
        self.cvs = take(SEG)
        self.uD = take(512)
        self.btok = take(NST)
        self.cbt = take(CH)
        self.mask01 = take(CH)
        self.ident = take(128)
        self.bc_w = take(D)
        self.bc_w2 = take(DIN)
        self.small = take(512)
        self.consts = take(16)
        self.convw = take(96)
        self.cvt = take(64)
        self.ffcol = take(48)
        self.ffs = take(16)
        self.dcol = take(4 * H)
        self.bcinvs = take(SEG)
        self.x_ab = take(D)
        assert o <= 22720, o
        self.dtp = self._sb("dtp", 32, 9 * SEG)
        self.arb = sb("arb", 128, 4096, BF16)
        b = self.arb
        self.aqT = b[:, 0:2048]
        self.yqT = b[:, 2048:4096]
        self.aq1T = b[:, 0:4096]
        self.hq = b[:, 0:4096]

        ps = lambda n: self.stack.enter_context(self.nc.psum_tensor(n, [128, 512], F32))
        self.ps_a = ps("ps_a")
        self.ps_z = ps("ps_z")
        self.ps_c = ps("ps_c")
        self.ps_y = ps("ps_y")
        self.ps_s = ps("ps_s")
        self.ps_t = ps("ps_t")
        self.ps_o = ps("ps_o")
        self.ps_o2 = ps("ps_o2")

    def bcast_row(self, stream, out2d, dram_row, ncols, name, reads=()):
        self.em.dma(
            "sync",
            stream,
            out2d.rearrange("p (a f) -> p a f", a=1),
            dram_row.unsqueeze(0).to_broadcast((128, 1, ncols)),
            reads=reads,
            writes=(name,),
        )

    def tiny(self, eng, fn, reads, writes):
        self.em.op(eng, fn, reads, writes)

    def _emit_program(self):
        em = self.em
        nc = self.nc

        em.op("gpsimd", lambda e: e.memset(_col(self.consts, 3), RC), writes=("rc_col",))
        em.op(
            "gpsimd",
            lambda e: e.iota(
                self.ident, pattern=[[1, 128]], base=0, channel_multiplier=-1,
                allow_small_or_imprecise_dtypes=True,
            ),
            writes=("ident",),
        )
        em.op(
            "vector",
            lambda e: e.tensor_scalar(
                out=self.mask01, in0=self.ident, scalar1=0.0, scalar2=None, op0=OP.is_ge
            ),
            reads=("ident",),
            writes=("mask01",),
        )
        em.op(
            "vector",
            lambda e: e.tensor_scalar(
                out=self.ident, in0=self.ident, scalar1=0.0, scalar2=None, op0=OP.is_equal
            ),
            reads=("ident",),
            writes=("ident",),
        )
        em.dma("sync", "b", self.bc_w.rearrange("p (a f) -> p a f", a=1),
               self.d_norm_w.unsqueeze(0).to_broadcast((128, 1, D)), writes=("bc_w",))
        em.dma("sync", "b", self.bc_w2.rearrange("p (a f) -> p a f", a=1),
               self.d_onw.unsqueeze(0).to_broadcast((128, 1, DIN)), writes=("bc_w2",))
        em.dma("sync", "b", self.consts[0:H, 0:1], self.d_alog, writes=("consts",))
        em.dma("sync", "b", self.consts[0:H, 1:2], self.d_dtb, writes=("consts",))
        em.op("scalar",
              lambda e: e.activation(out=self.consts[0:H, 2:3], in_=self.consts[0:H, 0:1], func=AF.Exp),
              reads=("consts",), writes=("consts",))
        em.op("vector",
              lambda e: e.tensor_scalar(out=self.consts[0:H, 2:3], in0=self.consts[0:H, 2:3],
                                        scalar1=-1.0, scalar2=None, op0=OP.mult),
              reads=("consts",), writes=("consts",))
        em.dma("sync", "b", self.convw[:, 0:72].rearrange("p (c k) -> p c k", c=18),
               self.d_conv_w.rearrange("(c p) k -> p c k", p=128), writes=("convw",))
        em.dma("sync", "b", self.convw[:, 72:90].rearrange("p (c k) -> p c k", k=1),
               self.d_conv_b.rearrange("(c p) k -> p c k", p=128), writes=("convw",),
               slow=True)
        em.dma("sync", "b", self.small[:, 352 : 352 + H].rearrange("p (a h) -> p a h", a=1),
               self.d_dp.unsqueeze(0).to_broadcast((128, 1, H)), writes=("smallDp",))
        em.dma("sync", "b", self.small[:, 320 : 320 + 32].rearrange("p (n k) -> p n k", k=1),
               self.d_b1.rearrange("(n p) k -> p n k", p=128), writes=("smallB1",),
               slow=True)

        self._quant_weights()
        if self.stop_after == "weights":
            self._final_drain()
            return

        self._attn_phase()
        if self.stop_after == "attn":
            self._final_drain()
            return

        self._ffn_phase()
        self._final_drain()

    def _final_drain(self):
        em = self.em
        em.phase_barrier()

    def _quant_weights(self):
        em = self.em
        sm = self.small

        slots = [
            (self.bcst[:, 0:2048], "wsl0"),
            (self.y_sb[:, 0:2048], "wsl1"),
            (self.x_sb[:, 0:2048], "wsl2"),
            (self.g_sb[:, 0:2048], "wsl3"),
            (self.xbc[:, 0:2048], "wsl4"),
            (self.xbc[:, 2048:4096], "wsl5"),
            (self.state[:, 0:2048], "wsl6"),
        ]
        gidx = [0]

        def nxt():
            i = gidx[0]
            gidx[0] += 1
            sl, sname = slots[i % 7]
            return sl, sname, f"wld{i % 7}", "sync"

        def tiles(K, N):
            for k in range(K // 128):
                for c in range((N + 2047) // 2048):
                    yield k, c, min(2048, N - c * 2048)

        def stat_pass(dram, K, N, mode, acc_base):
            idx = 0
            for k, c, cn in tiles(K, N):
                sl, sname, st, seng = nxt()
                stg = sl[:, 0:cn]
                em.dma(seng, st, stg,
                       dram[k * 128 : (k + 1) * 128, c * 2048 : c * 2048 + cn],
                       writes=(sname,))
                accc = _col(sm, acc_base + idx)
                if mode == "mean":
                    em.op("scalar",
                          lambda e, stg=stg, accc=accc: e.activation(
                              out=stg, in_=stg, func=AF.Abs, accum_out=accc),
                          reads=(sname,), writes=(sname, "smallacc"))
                else:
                    em.op("vector",
                          lambda e, stg=stg, accc=accc: e.tensor_reduce(
                              out=accc, in_=stg, axis=mybir.AxisListType.X, op=OP.max,
                              apply_absolute_value=True),
                          reads=(sname,), writes=("smallacc",))
                idx += 1
            return idx

        def fold_cols(nidx, acc_base, mode, out_col, fi):
            op = OP.add if mode == "mean" else OP.max
            red = _col(sm, acc_base + nidx)
            em.op("vector",
                  lambda e: e.tensor_reduce(out=red, in_=sm[:, acc_base : acc_base + nidx],
                                            axis=mybir.AxisListType.X, op=op),
                  reads=("smallacc",), writes=(f"smallred{fi}",))
            self._fold_partitions(red, f"smallred{fi}", op, acc_base + nidx + 1,
                                  f"smallfold{fi}")
            em.dma("gpsimd", "o", self.d_bn[0:1, fi : fi + 1],
                   sm[0:1, acc_base + nidx + 1 : acc_base + nidx + 2],
                   reads=(f"smallfold{fi}",), writes=(f"bnw{fi}",))
            self.bcast_row("b", _col(sm, out_col), self.d_bn[0:1, fi : fi + 1], 1,
                           f"smallc{out_col}", reads=(f"bnw{fi}",))

        def round_pass(dram, K, N, inv_col, dest_fn, clip1, spill=None):
            idx = 0
            for k, c, cn in tiles(K, N):
                sl, sname, st, seng = nxt()
                stg = sl[:, 0:cn]
                em.dma(seng, st, stg,
                       dram[k * 128 : (k + 1) * 128, c * 2048 : c * 2048 + cn],
                       writes=(sname,))
                em.op("scalar",
                      lambda e, stg=stg: e.activation(
                          out=stg, in_=stg, func=AF.Identity,
                          scale=_col(sm, inv_col), bias=_col(self.consts, 3)),
                      reads=(sname, f"smallc{inv_col}", "rc_col"), writes=(sname,))
                dest, dname = dest_fn(k, c, cn, idx)
                if clip1:
                    em.op("vector",
                          lambda e, stg=stg, dest=dest: e.tensor_scalar(
                              out=dest, in0=stg, scalar1=-RC, scalar2=1.0,
                              op0=OP.add, op1=OP.min),
                          reads=(sname,), writes=(dname,))
                    em.op("vector",
                          lambda e, dest=dest: e.tensor_scalar(
                              out=dest, in0=dest, scalar1=-1.0, scalar2=None, op0=OP.max),
                          reads=(dname,), writes=(dname,))
                else:
                    em.op("vector",
                          lambda e, stg=stg, dest=dest: e.tensor_scalar(
                              out=dest, in0=stg, scalar1=-RC, scalar2=None, op0=OP.add),
                          reads=(sname,), writes=(dname,))
                if spill is not None:
                    em.dma("gpsimd", f"osp{idx % 2}",
                           spill[k * 128 : (k + 1) * 128, c * 2048 : c * 2048 + cn],
                           dest, reads=(dname,))
                idx += 1

        def scalar_ops_mean(sum_col, KN, out_inv_col):
            c = _col(sm, sum_col)
            em.op("vector",
                  lambda e: e.tensor_scalar(out=c, in0=c, scalar1=1.0 / KN, scalar2=1e-5,
                                            op0=OP.mult, op1=OP.max),
                  reads=(f"smallc{sum_col}",), writes=(f"smallc{sum_col}",))
            ic = _col(sm, out_inv_col)
            em.op("vector", lambda e: e.reciprocal(out=ic, in_=c),
                  reads=(f"smallc{sum_col}",), writes=(f"smallc{out_inv_col}",))

        def scalar_ops_max(max_col, out_inv_col, sw_col):
            c = _col(sm, max_col)
            em.op("vector",
                  lambda e: e.tensor_scalar(out=_col(sm, sw_col), in0=c, scalar1=1e-6,
                                            scalar2=1.0 / 127.0, op0=OP.max, op1=OP.mult),
                  reads=(f"smallc{max_col}",), writes=(f"smallc{sw_col}",))
            em.op("vector", lambda e: e.reciprocal(out=_col(sm, out_inv_col), in_=_col(sm, sw_col)),
                  reads=(f"smallc{sw_col}",), writes=(f"smallc{out_inv_col}",))

        def dest_spill(k, c, cn, idx):
            return self.arb[:, (idx % 2) * 2048 : (idx % 2) * 2048 + cn], f"wqs{idx % 2}"

        def dest_win(k, c, cn, idx):
            return self.wq_in[:, k * NPROJ + c * 2048 : k * NPROJ + c * 2048 + cn], "wq_in"

        def dest_wout(k, c, cn, idx):
            return self.wq_out[:, k * D + c * 2048 : k * D + c * 2048 + cn], "wq_out"

        n1 = stat_pass(self.d_w1, D, DFF, "max", 0)
        n2 = stat_pass(self.d_w2, DFF, D, "max", 60)
        fold_cols(n1, 0, "max", 40, fi=0)
        scalar_ops_max(40, 41, 42)
        n3 = stat_pass(self.d_win, D, NPROJ, "mean", 100)
        fold_cols(n2, 60, "max", 43, fi=1)
        scalar_ops_max(43, 44, 45)
        n4 = stat_pass(self.d_wout, DIN, D, "mean", 130)
        fold_cols(n3, 100, "mean", 46, fi=2)
        scalar_ops_mean(46, D * NPROJ, 47)
        fold_cols(n4, 130, "mean", 48, fi=3)
        scalar_ops_mean(48, DIN * D, 49)
        round_pass(self.d_w1, D, DFF, 41, dest_spill, False, spill=self.d_w1q)
        round_pass(self.d_w2, DFF, D, 44, dest_spill, False, spill=self.d_w2q)
        round_pass(self.d_win, D, NPROJ, 47, dest_win, clip1=True)
        round_pass(self.d_wout, DIN, D, 49, dest_wout, clip1=True)

    def tap(self, ap, reads):
        if not DEBUG or self.ntap >= 64:
            return
        i = self.ntap
        self.ntap += 1
        p, w = ap.shape[0], ap.shape[-1]
        self.em.phase_barrier()
        self.em.dma("gpsimd", "o", self.d_dbg[0:p, i * 512 : i * 512 + w], ap, reads=reads,
                    slow=True)
        self.em.phase_barrier()

    def _fold_partitions(self, col, rname, op, tmp_i, out_name):
        em = self.em
        em.op("tensor",
              lambda e: e.matmul(self.ps_c[0:1, 0:128], col, self.ident, is_transpose=True),
              reads=(rname, "ident"), writes=("ps_c",))
        em.op("vector",
              lambda e: e.tensor_reduce(out=self.small[0:1, tmp_i : tmp_i + 1],
                                        in_=self.ps_c[0:1, 0:128],
                                        axis=mybir.AxisListType.X, op=op),
              reads=("ps_c",), writes=(out_name,))

    def _pe_round_T(self, src, srcnames, nk, dst_fn, dstname):
        em = self.em
        banks = ((self.ps_c, "ps_c"), (self.ps_t, "ps_t"),
                 (self.ps_a, "ps_a"), (self.ps_z, "ps_z"))
        for k in range(nk):
            pst, pname = banks[k % 4]
            em.op("tensor",
                  lambda e, k=k, pst=pst: e.matmul(
                      pst[:, 0:128], src[:, k * 128 : (k + 1) * 128], self.ident,
                      is_transpose=True),
                  reads=tuple(srcnames) + ("ident",), writes=(pname,))
            em.op("vector",
                  lambda e, k=k, pst=pst: e.tensor_scalar(
                      out=dst_fn(k), in0=pst[:, 0:128], scalar1=RC, scalar2=-RC,
                      op0=OP.add, op1=OP.add),
                  reads=(pname,), writes=(dstname,))

    def _rsqrt_col(self, out_c, in_c, eps, rnames, wname, dbg=False, tmp=120):
        em = self.em
        sm = self.small
        t_c = _col(sm, tmp)
        r_c = out_c
        em.op("vector",
              lambda e: e.tensor_scalar(out=t_c, in0=in_c, scalar1=eps, scalar2=None, op0=OP.add),
              reads=rnames, writes=(f"c{tmp}",))
        em.op("scalar", lambda e: e.activation(out=r_c, in_=t_c, func=AF.Sqrt),
              reads=(f"c{tmp}",), writes=(wname,))
        em.op("vector", lambda e: e.reciprocal(out=r_c, in_=r_c),
              reads=(wname,), writes=(wname,))

    def _ln_quant_prep(self, xi, scr1, scr2, scr1n, scr2n, xname, aq_dst, aq_name, cbase,
                       dbg=False, tpose=None, rtmp=120):
        em = self.em
        sm = self.small
        c = lambda i: _col(sm, cbase + i)
        cn = lambda i: f"c{cbase + i}"

        em.op("vector",
              lambda e: e.tensor_reduce(out=c(0), in_=xi, axis=mybir.AxisListType.X, op=OP.add),
              reads=(xname,), writes=(cn(0),))
        em.op("scalar",
              lambda e: e.activation(out=scr1, in_=xi, func=AF.Square, accum_out=c(1)),
              reads=(xname,), writes=(scr1n, cn(1)))
        em.op("vector", lambda e: e.tensor_scalar(out=c(0), in0=c(0), scalar1=1.0 / D,
                                                  scalar2=None, op0=OP.mult),
              reads=(cn(0),), writes=(cn(0),))
        em.op("scalar", lambda e: e.activation(out=c(2), in_=c(0), func=AF.Square),
              reads=(cn(0),), writes=(cn(2),))
        em.op("vector",
              lambda e: e.scalar_tensor_tensor(out=c(1), in0=sm[:, cbase + 1 : cbase + 2],
                                               scalar=1.0 / D, in1=c(2), op0=OP.mult,
                                               op1=OP.subtract),
              reads=(cn(1), cn(2)), writes=(cn(1),))
        if dbg:
            self.tap(self.small[:, cbase : cbase + 12], (cn(0), cn(1), cn(2)))
        self._rsqrt_col(c(3), c(1), EPS_LN, (cn(1),), cn(3), dbg=dbg, tmp=rtmp)
        if dbg:
            self.tap(self.small[:, cbase : cbase + 12], (cn(3),))
        em.op("scalar", lambda e: e.activation(out=c(4), in_=c(3), func=AF.Square),
              reads=(cn(3),), writes=(cn(4),))
        em.op("vector", lambda e: e.tensor_tensor(out=c(4), in0=c(4), in1=c(1), op=OP.mult),
              reads=(cn(4), cn(1)), writes=(cn(4),))
        self._rsqrt_col(c(5), c(4), EPS_RMS, (cn(4),), cn(5), tmp=rtmp)
        em.op("vector", lambda e: e.tensor_tensor(out=c(5), in0=c(5), in1=c(3), op=OP.mult),
              reads=(cn(5), cn(3)), writes=(cn(5),))
        em.op("vector",
              lambda e: e.scalar_tensor_tensor(out=c(6), in0=c(0), scalar=-1.0, in1=c(5),
                                               op0=OP.mult, op1=OP.mult),
              reads=(cn(0), cn(5)), writes=(cn(6),))
        em.op("scalar",
              lambda e: e.activation(out=scr1, in_=xi, func=AF.Identity, scale=c(5), bias=c(6)),
              reads=(xname, cn(5), cn(6)), writes=(scr1n,))
        em.op("vector",
              lambda e: e.scalar_tensor_tensor(out=scr2, in0=scr1, scalar=1.0, in1=self.bc_w,
                                               op0=OP.bypass, op1=OP.mult, accum_out=c(7)),
              reads=(scr1n, "bc_w"), writes=(scr2n, cn(7)))
        if dbg:
            self.tap(scr1[:, 0:512], (scr1n,))
            self.tap(scr2[:, 0:512], (scr2n,))
        em.op("scalar",
              lambda e: e.activation(out=scr1, in_=scr2, func=AF.Square, accum_out=c(8)),
              reads=(scr2n,), writes=(scr1n, cn(8)))
        em.op("vector", lambda e: e.tensor_scalar(out=c(7), in0=c(7), scalar1=1.0 / D,
                                                  scalar2=None, op0=OP.mult),
              reads=(cn(7),), writes=(cn(7),))
        em.op("scalar", lambda e: e.activation(out=c(2), in_=c(7), func=AF.Square),
              reads=(cn(7),), writes=(cn(2),))
        em.op("vector",
              lambda e: e.scalar_tensor_tensor(out=c(8), in0=sm[:, cbase + 8 : cbase + 9],
                                               scalar=1.0 / D, in1=c(2), op0=OP.mult,
                                               op1=OP.subtract),
              reads=(cn(8), cn(2)), writes=(cn(8),))
        self._rsqrt_col(c(9), c(8), EPS_LN, (cn(8),), cn(9), tmp=rtmp)
        em.op("vector",
              lambda e: e.tensor_reduce(out=c(2), in_=scr2, axis=mybir.AxisListType.X, op=OP.max),
              reads=(scr2n,), writes=(cn(2),))
        em.op("vector",
              lambda e: e.tensor_reduce(out=c(1), in_=scr2, axis=mybir.AxisListType.X, op=OP.min),
              reads=(scr2n,), writes=(cn(1),))
        em.op("vector", lambda e: e.tensor_tensor(out=c(2), in0=c(2), in1=c(7), op=OP.subtract),
              reads=(cn(2), cn(7)), writes=(cn(2),))
        em.op("vector", lambda e: e.tensor_tensor(out=c(1), in0=c(7), in1=c(1), op=OP.subtract),
              reads=(cn(1), cn(7)), writes=(cn(1),))
        em.op("vector", lambda e: e.tensor_tensor(out=c(2), in0=c(2), in1=c(1), op=OP.max),
              reads=(cn(2), cn(1)), writes=(cn(2),))
        em.op("vector", lambda e: e.tensor_tensor(out=c(2), in0=c(2), in1=c(9), op=OP.mult),
              reads=(cn(2), cn(9)), writes=(cn(2),))
        em.op("vector", lambda e: e.tensor_scalar(out=c(2), in0=c(2), scalar1=1e-5,
                                                  scalar2=None, op0=OP.max),
              reads=(cn(2),), writes=(cn(2),))
        em.op("vector", lambda e: e.tensor_scalar(out=c(10), in0=c(2), scalar1=1.0 / 127.0,
                                                  scalar2=None, op0=OP.mult),
              reads=(cn(2),), writes=(cn(10),))
        em.op("vector", lambda e: e.reciprocal(out=c(11), in_=c(10)),
              reads=(cn(10),), writes=(cn(11),))
        em.op("vector", lambda e: e.tensor_tensor(out=c(9), in0=c(9), in1=c(11), op=OP.mult),
              reads=(cn(9), cn(11)), writes=(cn(9),))
        em.op("vector",
              lambda e: e.scalar_tensor_tensor(out=c(6), in0=c(7), scalar=-1.0, in1=c(9),
                                               op0=OP.mult, op1=OP.mult),
              reads=(cn(7), cn(9)), writes=(cn(6),))
        em.op("scalar",
              lambda e: e.activation(out=scr1, in_=scr2, func=AF.Identity, scale=c(9), bias=c(6)),
              reads=(scr2n, cn(9), cn(6)), writes=(scr1n,))
        if tpose is not None:
            nk, dst_fn, dstname = tpose
            self._pe_round_T(scr1, (scr1n,), nk, dst_fn, dstname)
        else:
            em.op("vector",
                  lambda e: e.tensor_scalar(out=aq_dst, in0=scr1, scalar1=RC, scalar2=-RC,
                                            op0=OP.add, op1=OP.add),
                  reads=(scr1n,), writes=(aq_name,))

    def _attn_phase(self):
        em = self.em
        sm = self.small
        em.phase_barrier()

        em.op("gpsimd", lambda e: e.memset(self.cvt, 0.0), writes=("cvt",))
        em.op("gpsimd", lambda e: e.memset(self.state, 0.0), writes=("state",))
        em.op("gpsimd", lambda e: e.memset(_col(sm, 200), 0.0), writes=("c200",))

        self._load_x(0)
        self._prep_segment(0)
        if self.stop_after == "prep0":
            self._final_drain()
            return
        for s in range(NSEG):
            self._seg_body(s)
            self._chunk(s, 0, do_h14=True)
            if s + 1 < NSEG:
                self._load_x(s + 1)
            self._chunk(s, 1, do_h14=False)
            if s + 1 < NSEG:
                self._prep_segment(s + 1)
            self._h14(s, 1)

        self._ffn_scale_collective()

    def _xtile(self, s, i):
        if i == 0:
            return self.x_sb[:, 0:D], "xs0"
        par = s % 2
        return (self.x_sb[:, D : 2 * D] if par == 0 else self.x_ab[:, 0:D]), f"xs1_{par}"

    def _load_x(self, s):
        t0 = s * SEG
        for i in range(2):
            xt, xn = self._xtile(s, i)
            stream = "xa" if i == 0 else f"xb{s % 2}"
            self.em.dma("sync", stream, xt,
                        self.dx[t0 + i * 128 : t0 + (i + 1) * 128, :],
                        writes=(xn,))

    def _prep_segment(self, s):
        em = self.em
        sm = self.small
        par = s % 2
        cb0 = 50 * par
        scr1 = self.g_sb[:, 0:D]
        scr2 = self.g_sb[:, 1024:2048]
        for i in range(2):
            xi, xname = self._xtile(s, i)
            dst_fn = (lambda i: lambda k:
                      self.aqT[:, k * SEG + i * 128 : k * SEG + i * 128 + 128])(i)
            self._ln_quant_prep(xi, scr1, scr2, "g_sb", "g_sb", xname,
                                None, None, cbase=cb0 + i * 12,
                                dbg=(DEBUG and s == 0 and i == 0),
                                tpose=(8, dst_fn, "aqT"), rtmp=120 + par)
        for i in range(2):
            em.dma("gpsimd", "o",
                   self.d_bn[par : par + 1, 256 + i * 128 : 256 + i * 128 + 128],
                   _col(sm, cb0 + i * 12 + 10), reads=(f"c{cb0 + i * 12 + 10}",),
                   writes=(f"bn_invs{par}",))
        em.dma("sync", "b",
               self.bcinvs.rearrange("p (a f) -> p a f", a=1),
               self.d_bn[par : par + 1, 256 : 256 + SEG]
                   .unsqueeze(0).to_broadcast((128, 1, SEG)),
               reads=(f"bn_invs{par}",), writes=("bc_invs",))

    def _seg_body(self, s):
        em = self.em
        sm = self.small
        for n in range(19):
            nb = min(128, XBCDT - n * 128)
            base = n * (SEG + 4)
            psA = self.ps_a if n % 2 == 0 else self.ps_t
            psAn = "ps_a" if n % 2 == 0 else "ps_t"
            if n < 18:
                em.op("vector",
                      lambda e, base=base, n=n: e.tensor_copy(
                          out=self.xbc[:, base : base + 3], in_=self.cvt[:, n * 3 : n * 3 + 3]),
                      reads=("cvt",), writes=("xbc",))
            for k in range(8):
                em.op("tensor",
                      lambda e, n=n, nb=nb, k=k, psA=psA: e.matmul(
                          psA[0:nb, 0:SEG],
                          self.wq_in[:, k * NPROJ + DIN + n * 128 : k * NPROJ + DIN + n * 128 + nb],
                          self.aqT[:, k * SEG : (k + 1) * SEG],
                          start=(k == 0), stop=(k == 7)),
                      reads=("wq_in", "aqT"), writes=(psAn,))
            em.op("vector",
                  lambda e, base=base, nb=nb, psA=psA: e.tensor_tensor(
                      out=self.xbc[0:nb, base + 3 : base + 3 + SEG],
                      in0=psA[0:nb, 0:SEG], in1=self.bcinvs[0:nb, :], op=OP.mult),
                  reads=(psAn, "bc_invs"), writes=("xbc",))
        if s == 0 and DEBUG:
            em.op("vector", lambda e: e.tensor_copy(out=self.cvs, in_=self.wq_in[:, 2048:2304]),
                  reads=("wq_in",), writes=("cvs",))
            self.tap(self.cvs, ("cvs",))
            self.tap(self.small[:, 0:49], ("c0", "c3", "c10", "c11", "smallc41", "smallc42", "smallc46", "smallc47"))
            self.tap(self.xbc[:, 3 : 3 + 256], ("xbc",))
            self.tap(self.bcinvs, ("bc_invs",))
        for n in range(18):
            base = n * (SEG + 4)
            em.op("vector",
                  lambda e, base=base, n=n: e.tensor_copy(
                      out=self.cvt[:, n * 3 : n * 3 + 3],
                      in_=self.xbc[:, base + SEG : base + SEG + 3]),
                  reads=("xbc",), writes=("cvt",))
            em.op("scalar",
                  lambda e, base=base, n=n: e.activation(
                      out=self.cvs, in_=self.xbc[:, base : base + SEG],
                      func=AF.Identity, scale=_col(self.convw, n * 4),
                      bias=_col(self.convw, 72 + n)),
                  reads=("xbc", "convw"), writes=("cvs",))
            em.op("vector",
                  lambda e, base=base, n=n: e.scalar_tensor_tensor(
                      out=self.cvs, in0=self.xbc[:, base + 1 : base + 1 + SEG],
                      scalar=_col(self.convw, n * 4 + 1), in1=self.cvs,
                      op0=OP.mult, op1=OP.add),
                  reads=("xbc", "convw", "cvs"), writes=("cvs",))
            em.op("vector",
                  lambda e, base=base, n=n: e.scalar_tensor_tensor(
                      out=self.cvs, in0=self.xbc[:, base + 2 : base + 2 + SEG],
                      scalar=_col(self.convw, n * 4 + 2), in1=self.cvs,
                      op0=OP.mult, op1=OP.add),
                  reads=("xbc", "convw", "cvs"), writes=("cvs",))
            em.op("vector",
                  lambda e, base=base, n=n: e.scalar_tensor_tensor(
                      out=self.cvs, in0=self.xbc[:, base + 3 : base + 3 + SEG],
                      scalar=_col(self.convw, n * 4 + 3), in1=self.cvs,
                      op0=OP.mult, op1=OP.add),
                  reads=("xbc", "convw", "cvs"), writes=("cvs",))
            em.op("scalar",
                  lambda e, base=base, n=n: e.activation(
                      out=self.xbc[:, base + 3 : base + 3 + SEG], in_=self.cvs,
                      func=AF.Silu),
                  reads=("cvs",), writes=("xbc",))
        if s == 0:
            self.tap(self.xbc[:, 3 : 3 + 256], ("xbc",))
            self.tap(self.xbc[:, 16 * (SEG + 4) + 3 : 16 * (SEG + 4) + 3 + 256], ("xbc",))
        self._dt_pipeline(s)
        if s == 0:
            self.tap(self.dtp[:, 2 * SEG : 3 * SEG], ("dt2",))
            self.tap(self.dtp[:, 5 * SEG : 6 * SEG], ("dt5",))
            self.tap(self.dtp[:, 6 * SEG : 7 * SEG], ("dt6",))
            self.tap(self.dtp[:, 7 * SEG : 8 * SEG], ("dt7",))

    def _dt_pipeline(self, s):
        em = self.em
        dtp = self.dtp
        SL = SEG

        def sl(i):
            return dtp[:, i * SL : (i + 1) * SL]

        raw = self.xbc[0:H, 18 * (SEG + 4) + 3 : 18 * (SEG + 4) + 3 + SEG]
        dtb = _col(self.consts, 1)[0:H, :]
        negA = _col(self.consts, 2)[0:H, :]
        em.op("scalar",
              lambda e: e.activation(out=sl(0), in_=raw, func=AF.Abs, bias=dtb),
              reads=("xbc", "consts"), writes=("dt0",))
        em.op("vector",
              lambda e: e.tensor_scalar(out=sl(1), in0=raw, scalar1=dtb, scalar2=0.0,
                                        op0=OP.add, op1=OP.max),
              reads=("xbc", "consts"), writes=("dt1",))
        em.op("scalar", lambda e: e.activation(out=sl(0), in_=sl(0), func=AF.Exp, scale=-1.0),
              reads=("dt0",), writes=("dt0",))
        em.op("vector",
              lambda e: e.tensor_scalar(out=sl(2), in0=sl(0), scalar1=-1.0 / 7, scalar2=1.0 / 6,
                                        op0=OP.mult, op1=OP.add),
              reads=("dt0",), writes=("dt2",))
        for kk, c in ((5, 1.0 / 5), (4, 1.0 / 4), (3, 1.0 / 3), (2, 1.0 / 2), (1, 1.0)):
            em.op("vector", lambda e: e.tensor_tensor(out=sl(3), in0=sl(0), in1=sl(2), op=OP.mult),
                  reads=("dt0", "dt2"), writes=("dt3",))
            em.op("vector",
                  lambda e, c=c: e.tensor_scalar(out=sl(2), in0=sl(3), scalar1=-1.0, scalar2=c,
                                                 op0=OP.mult, op1=OP.add),
                  reads=("dt3",), writes=("dt2",))
        em.op("vector", lambda e: e.tensor_tensor(out=sl(2), in0=sl(0), in1=sl(2), op=OP.mult),
              reads=("dt0", "dt2"), writes=("dt2",))
        em.op("vector", lambda e: e.tensor_scalar(out=sl(3), in0=sl(0), scalar1=1.0,
                                                  scalar2=None, op0=OP.add),
              reads=("dt0",), writes=("dt3",))
        em.op("scalar", lambda e: e.activation(out=sl(3), in_=sl(3), func=AF.Ln),
              reads=("dt3",), writes=("dt3",))
        em.op("vector", lambda e: e.tensor_scalar(out=sl(4), in0=sl(0), scalar1=0.368,
                                                  scalar2=None, op0=OP.is_gt),
              reads=("dt0",), writes=("dt4",))
        em.op("vector", lambda e: e.tensor_tensor(out=sl(3), in0=sl(3), in1=sl(2), op=OP.subtract),
              reads=("dt3", "dt2"), writes=("dt3",))
        em.op("vector", lambda e: e.tensor_tensor(out=sl(3), in0=sl(3), in1=sl(4), op=OP.mult),
              reads=("dt3", "dt4"), writes=("dt3",))
        em.op("vector", lambda e: e.tensor_tensor(out=sl(2), in0=sl(2), in1=sl(3), op=OP.add),
              reads=("dt2", "dt3"), writes=("dt2",))
        em.op("vector", lambda e: e.tensor_tensor(out=sl(2), in0=sl(2), in1=sl(1), op=OP.add),
              reads=("dt2", "dt1"), writes=("dt2",))
        em.op("scalar", lambda e: e.activation(out=sl(3), in_=sl(2), func=AF.Ln),
              reads=("dt2",), writes=("dt3",))
        em.op("vector", lambda e: e.tensor_scalar(out=sl(3), in0=sl(3), scalar1=-60000.0,
                                                  scalar2=None, op0=OP.max),
              reads=("dt3",), writes=("dt3",))
        em.op("vector", lambda e: e.tensor_scalar(out=sl(4), in0=sl(2), scalar1=negA,
                                                  scalar2=None, op0=OP.mult),
              reads=("dt2", "consts"), writes=("dt4",))
        for c01 in range(2):
            a_sl = dtp[:, 4 * SL + c01 * CH : 4 * SL + (c01 + 1) * CH]
            acs_sl = dtp[:, 5 * SL + c01 * CH : 5 * SL + (c01 + 1) * CH]
            em.op("vector",
                  lambda e, a_sl=a_sl, acs_sl=acs_sl: e.tensor_tensor_scan(
                      out=acs_sl, data0=a_sl, data1=a_sl, initial=0.0,
                      op0=OP.add, op1=OP.bypass),
                  reads=("dt4",), writes=("dt5",))
            dec_sl = dtp[:, 6 * SL + c01 * CH : 6 * SL + (c01 + 1) * CH]
            em.op("scalar",
                  lambda e, acs_sl=acs_sl, dec_sl=dec_sl, c01=c01: e.activation(
                      out=dec_sl, in_=acs_sl, func=AF.Exp, scale=-1.0,
                      bias=dtp[:, 5 * SL + c01 * CH + CH - 1 : 5 * SL + c01 * CH + CH]),
                  reads=("dt5",), writes=("dt6",))
            em.op("vector",
                  lambda e, dec_sl=dec_sl, c01=c01: e.tensor_tensor(
                      out=dec_sl, in0=dec_sl,
                      in1=dtp[:, 2 * SL + c01 * CH : 2 * SL + (c01 + 1) * CH], op=OP.mult),
                  reads=("dt6", "dt2"), writes=("dt6",))
            cds = dtp[:, 7 * SL + c01 * CH : 7 * SL + (c01 + 1) * CH]
            em.op("vector",
                  lambda e, cds=cds, acs_sl=acs_sl, c01=c01: e.tensor_tensor(
                      out=cds, in0=dtp[:, 3 * SL + c01 * CH : 3 * SL + (c01 + 1) * CH],
                      in1=acs_sl, op=OP.subtract),
                  reads=("dt3", "dt5"), writes=("dt7",))
            em.op("scalar",
                  lambda e, c01=c01: e.activation(
                      out=dtp[:, 8 * SL + c01 : 8 * SL + c01 + 1],
                      in_=dtp[:, 5 * SL + c01 * CH + CH - 1 : 5 * SL + c01 * CH + CH],
                      func=AF.Exp),
                  reads=("dt5",), writes=("dt8",))
        em.dma("gpsimd", "o", self.d_bn[32:64, 0:2],
               self.dtp[:, 8 * SL : 8 * SL + 2], reads=("dt8",), writes=("bn_d",))
        em.dma("sync", "b",
               self.dcol[:, 2 * H : 2 * H + 2 * H].rearrange("p (h c) -> p h c", c=2),
               self.d_bn[32:64, 0:2].unsqueeze(0).to_broadcast((128, H, 2)),
               reads=("bn_d",), writes=("dcol_d",))

    def _chunk(self, s, c01, do_h14=True):
        em = self.em
        sm = self.small
        SL = SEG
        dtp = self.dtp
        PHG = 8
        cb0 = 50 * (s % 2)

        invs_c = _col(sm, cb0 + c01 * 12 + 10)
        for nzg in range(2):
            for k in range(8):
                for j in range(2):
                    nz = nzg * 2 + j
                    pst = self.ps_z if j == 0 else self.ps_o2
                    em.op("tensor",
                          lambda e, k=k, nz=nz, pst=pst: e.matmul(
                              pst[:, 0:512],
                              self.aqT[:, k * SEG + c01 * 128 : k * SEG + c01 * 128 + 128],
                              self.wq_in[:, k * NPROJ + nz * 512 : k * NPROJ + nz * 512 + 512],
                              start=(k == 0), stop=(k == 7)),
                          reads=("aqT", "wq_in"), writes=("ps_z" if j == 0 else "ps_o2",))
            for j in range(2):
                nz = nzg * 2 + j
                pst = self.ps_z if j == 0 else self.ps_o2
                em.op("scalar",
                      lambda e, nz=nz, pst=pst: e.activation(
                          out=self.g_sb[:, nz * 512 : nz * 512 + 512], in_=pst[:, 0:512],
                          func=AF.Silu, scale=invs_c),
                      reads=(("ps_z" if j == 0 else "ps_o2"), f"c{cb0 + c01*12+10}"),
                      writes=("g_sb",))

        cds = dtp[:, 7 * SL + c01 * CH : 7 * SL + (c01 + 1) * CH]
        dec_sl = dtp[:, 6 * SL + c01 * CH : 6 * SL + (c01 + 1) * CH]
        em.op("tensor",
              lambda e: e.matmul(self.ps_c[:, 0:H], cds, self.ident[0:H, 0:H], is_transpose=True),
              reads=("dt7", "ident"), writes=("ps_c",))
        em.op("vector", lambda e: e.tensor_copy(out=self.dcol[:, 0:H], in_=self.ps_c[:, 0:H]),
              reads=("ps_c",), writes=("dcol",))
        em.op("tensor",
              lambda e: e.matmul(self.ps_c[:, 0:H], dec_sl, self.ident[0:H, 0:H], is_transpose=True),
              reads=("dt6", "ident", "dcol"), writes=("ps_c",))
        em.op("vector", lambda e: e.tensor_copy(out=self.dcol[:, H : 2 * H], in_=self.ps_c[:, 0:H]),
              reads=("ps_c",), writes=("dcol",))

        bslice = self.xbc[:, 16 * (SEG + 4) + 3 + c01 * CH : 16 * (SEG + 4) + 3 + (c01 + 1) * CH]
        cslice = self.xbc[:, 17 * (SEG + 4) + 3 + c01 * CH : 17 * (SEG + 4) + 3 + (c01 + 1) * CH]
        em.op("tensor",
              lambda e: e.matmul(self.ps_c[:, 128:256], bslice, cslice),
              reads=("xbc", "dcol"), writes=("ps_c",))
        em.op("vector",
              lambda e: e.tensor_tensor(out=self.cbt, in0=self.ps_c[:, 128:256],
                                        in1=self.mask01, op=OP.mult),
              reads=("ps_c", "mask01"), writes=("cbt",))
        em.op("tensor",
              lambda e: e.matmul(self.ps_c[:, 256:384], bslice, self.ident, is_transpose=True),
              reads=("xbc", "ident", "cbt"), writes=("ps_c",))
        em.op("vector", lambda e: e.tensor_copy(out=self.btok, in_=self.ps_c[:, 256:384]),
              reads=("ps_c",), writes=("btok",))

        acs_sl = dtp[:, 5 * SL + c01 * CH : 5 * SL + (c01 + 1) * CH]
        em.dma("gpsimd", "o", self.d_bn[0:H, 0:CH], acs_sl, reads=("dt5",), writes=("bn_acs",))
        em.op("tensor",
              lambda e: e.matmul(self.ps_y[:, 0:H], acs_sl, self.ident[0:H, 0:H],
                                 is_transpose=True),
              reads=("dt5", "ident"), writes=("ps_y",))
        em.op("scalar",
              lambda e: e.activation(out=sm[:, 148 : 148 + H], in_=self.ps_y[:, 0:H],
                                     func=AF.Exp),
              reads=("ps_y",), writes=("expacsT",))

        def utrans(half):
            h0 = half * 16
            for jg in range(2):
                for j4 in range(4):
                    j = h0 // 2 + jg * 4 + j4
                    src = self.xbc[:, j * (SEG + 4) + 3 + c01 * CH : j * (SEG + 4) + 3 + (c01 + 1) * CH]
                    em.op("tensor",
                          lambda e, j4=j4, src=src: e.matmul(
                              self.ps_t[:, j4 * 128 : j4 * 128 + 128], src, self.ident,
                              is_transpose=True),
                          reads=("xbc", "ident", "u_tok"), writes=("ps_t",))
                em.op("vector",
                      lambda e, jg=jg: e.tensor_copy(
                          out=self.u_tok[:, jg * 512 : jg * 512 + 512], in_=self.ps_t[:, 0:512]),
                      reads=("ps_t",), writes=("u_tok",))

        def stg(gq):
            return self.bcst[:, (gq % 2) * 1024 : (gq % 2) * 1024 + 1024], f"bch{gq % 2}"

        def banks(gq):
            return ((self.ps_y, "ps_y") if gq % 2 == 0 else (self.ps_z, "ps_z"),
                    (self.ps_s, "ps_s") if gq % 2 == 0 else (self.ps_o, "ps_o"),
                    (self.ps_c, "ps_c") if gq % 2 == 0 else (self.ps_t, "ps_t"))

        def schain(gq):
            qh0 = gq * 8
            stage, stgname = stg(gq)
            em.dma("sync", f"b{gq % 2}",
                   stage.rearrange("p (h l) -> p h l", h=PHG),
                   self.d_bn[qh0 : qh0 + PHG, 0:CH].unsqueeze(0).to_broadcast((128, PHG, CH)),
                   reads=("bn_acs",), writes=(stgname,))
            em.op("vector",
                  lambda e, stage=stage, qh0=qh0: e.tensor_tensor(
                      out=stage.rearrange("p (h l) -> p h l", h=PHG),
                      in0=stage.rearrange("p (h l) -> p h l", h=PHG),
                      in1=self.dcol[:, qh0 : qh0 + PHG].unsqueeze(2).broadcast_to(
                          [128, PHG, CH]),
                      op=OP.add),
                  reads=(stgname, "dcol"), writes=(stgname,))
            em.op("vector",
                  lambda e, stage=stage: e.tensor_scalar(
                      out=stage, in0=stage, scalar1=50.0, scalar2=None, op0=OP.min),
                  reads=(stgname,), writes=(stgname,))
            em.op("scalar",
                  lambda e, stage=stage: e.activation(out=stage, in_=stage, func=AF.Exp),
                  reads=(stgname,), writes=(stgname,))
            em.op("vector",
                  lambda e, stage=stage: e.tensor_tensor(
                      out=stage.rearrange("p (h l) -> p h l", h=PHG),
                      in0=stage.rearrange("p (h l) -> p h l", h=PHG),
                      in1=self.cbt.unsqueeze(1).broadcast_to([128, PHG, CH]), op=OP.mult),
                  reads=(stgname, "cbt"), writes=(stgname,))

        def ud_op(gq):
            qh0 = gq * 8
            uoff = (gq % 2) * 512
            em.op("vector",
                  lambda e, uoff=uoff, qh0=qh0: e.tensor_tensor(
                      out=self.uD.rearrange("p (h q) -> p h q", h=PHG),
                      in0=self.u_tok[:, uoff : uoff + 512].rearrange(
                          "p (h q) -> p h q", h=PHG),
                      in1=self.dcol[:, H + qh0 : H + qh0 + PHG].unsqueeze(2).broadcast_to(
                          [128, PHG, PH]),
                      op=OP.mult),
                  reads=("u_tok", "dcol"), writes=("uD",))

        def mms(gq):
            qh0 = gq * 8
            h0 = (gq // 2) * 16
            stage, stgname = stg(gq)
            (psY, psYn), (psS, psSn), (psC, psCn) = banks(gq)
            for hh in range(PHG):
                h = qh0 + hh
                ul = (h - h0) * PH
                em.op("tensor",
                      lambda e, hh=hh, ul=ul, stage=stage, psY=psY: e.matmul(
                          psY[:, hh * PH : (hh + 1) * PH],
                          stage[:, hh * CH : (hh + 1) * CH],
                          self.u_tok[:, ul : ul + PH],
                          start=True, stop=True),
                      reads=(stgname, "u_tok"), writes=(psYn,))
            em.op("tensor",
                  lambda e, qh0=qh0, psC=psC: e.matmul(
                      psC[:, 0:512], cslice,
                      self.state[:, qh0 * PH : qh0 * PH + 512],
                      start=True, stop=True),
                  reads=("xbc", "state"), writes=(psCn,))
            em.op("tensor",
                  lambda e, psS=psS: e.matmul(psS[:, 0:512], self.btok, self.uD,
                                              start=True, stop=True),
                  reads=("btok", "uD"), writes=(psSn,))

        def evacs(gq):
            qh0 = gq * 8
            (psY, psYn), (psS, psSn), (psC, psCn) = banks(gq)
            em.op("vector",
                  lambda e, uoff=(gq % 2) * 512, qh0=qh0: e.tensor_tensor(
                      out=self.y_sb[:, qh0 * PH : qh0 * PH + 512].rearrange(
                          "p (h q) -> p h q", h=PHG),
                      in0=self.u_tok[:, uoff : uoff + 512].rearrange(
                          "p (h q) -> p h q", h=PHG),
                      in1=sm[:, 352 + qh0 : 352 + qh0 + PHG].unsqueeze(2).broadcast_to(
                          [128, PHG, PH]),
                      op=OP.mult),
                  reads=("u_tok", "smallDp"), writes=("y_sb",))
            em.op("vector",
                  lambda e, qh0=qh0, psY=psY: e.tensor_tensor(
                      out=self.y_sb[:, qh0 * PH : qh0 * PH + 512],
                      in0=self.y_sb[:, qh0 * PH : qh0 * PH + 512],
                      in1=psY[:, 0:512], op=OP.add),
                  reads=("y_sb", psYn), writes=("y_sb",))
            em.op("vector",
                  lambda e, qh0=qh0, psC=psC: e.tensor_tensor(
                      out=psC[:, 0:512].rearrange("p (h q) -> p h q", h=PHG),
                      in0=psC[:, 0:512].rearrange("p (h q) -> p h q", h=PHG),
                      in1=sm[:, 148 + qh0 : 148 + qh0 + PHG].unsqueeze(2).broadcast_to(
                          [128, PHG, PH]),
                      op=OP.mult),
                  reads=(psCn, "expacsT"), writes=(psCn,))
            em.op("vector",
                  lambda e, qh0=qh0, psC=psC: e.tensor_tensor(
                      out=self.y_sb[:, qh0 * PH : qh0 * PH + 512],
                      in0=self.y_sb[:, qh0 * PH : qh0 * PH + 512],
                      in1=psC[:, 0:512], op=OP.add),
                  reads=("y_sb", psCn), writes=("y_sb",))
            em.op("vector",
                  lambda e, qh0=qh0: e.tensor_tensor(
                      out=self.y_sb[:, qh0 * PH : qh0 * PH + 512],
                      in0=self.y_sb[:, qh0 * PH : qh0 * PH + 512],
                      in1=self.g_sb[:, qh0 * PH : qh0 * PH + 512], op=OP.mult),
                  reads=("y_sb", "g_sb"), writes=("y_sb",))
            em.op("vector",
                  lambda e, qh0=qh0: e.tensor_tensor(
                      out=self.state[:, qh0 * PH : qh0 * PH + 512].rearrange(
                          "p (h q) -> p h q", h=PHG),
                      in0=self.state[:, qh0 * PH : qh0 * PH + 512].rearrange(
                          "p (h q) -> p h q", h=PHG),
                      in1=self.dcol[:, 2 * H : 4 * H]
                          .rearrange("p (h c) -> p h c", c=2)[:, qh0 : qh0 + PHG,
                                                             c01 : c01 + 1]
                          .broadcast_to([128, PHG, PH]),
                      op=OP.mult),
                  reads=("state", "dcol_d"), writes=("state",))
            em.op("vector",
                  lambda e, qh0=qh0, psS=psS: e.tensor_tensor(
                      out=self.state[:, qh0 * PH : qh0 * PH + 512],
                      in0=self.state[:, qh0 * PH : qh0 * PH + 512],
                      in1=psS[:, 0:512], op=OP.add),
                  reads=("state", psSn), writes=("state",))

        utrans(0)
        schain(0)
        ud_op(0)
        mms(0)
        schain(1)
        evacs(0)
        ud_op(1)
        mms(1)
        schain(2)
        evacs(1)
        utrans(1)
        ud_op(2)
        mms(2)
        schain(3)
        evacs(2)
        ud_op(3)
        mms(3)
        evacs(3)

        if s == 0 and c01 == 0:
            self.tap(self.g_sb[:, 0:512], ("g_sb",))
            self.tap(self.u_tok[:, 512:1024], ("u_tok",))
            self.tap(self.cbt, ("cbt",))
            self.tap(self.y_sb[:, 0:512], ("y_sb",))
            self.tap(self.state[:, 0:512], ("state",))
            self.tap(self.dcol[:, 0:128], ("dcol",))
        self._out_prep(c01)
        if DEBUG and s == 0 and c01 == 0:
            self.tap(self.bcst[:, 0:512], ("bch0", "bch1"))
            for _t in range(4):
                self.tap(self.y_sb[:, _t * 512 : (_t + 1) * 512], ("y_sb",))
            em.op("vector", lambda e: e.tensor_copy(out=self.uD, in_=self.wq_out[:, 0:512]),
                  reads=("wq_out",), writes=("uD",))
            self.tap(self.uD, ("uD",))
        for no in range(2):
            pst = self.ps_o if no == 0 else self.ps_o2
            pname = "ps_o" if no == 0 else "ps_o2"
            for k in range(16):
                em.op("tensor",
                      lambda e, k=k, no=no, pst=pst: e.matmul(
                          pst[:, 0:512], self.yqT[:, k * CH : (k + 1) * CH],
                          self.wq_out[:, k * D + no * 512 : k * D + no * 512 + 512],
                          start=(k == 0), stop=(k == 15)),
                      reads=("yqT", "wq_out"), writes=(pname,))
        if do_h14:
            self._h14(s, c01)

    def _h14(self, s, c01):
        em = self.em
        sm = self.small
        xm, xname = self._xtile(s, c01)
        invs2 = _col(sm, 37)
        cb1 = 50 * (s % 2) + c01 * 12
        em.op("vector",
              lambda e: e.tensor_scalar(out=_col(sm, 215), in0=_col(sm, cb1 + 3),
                                        scalar1=1.0, scalar2=None, op0=OP.add),
              reads=(f"c{cb1+3}",), writes=("c215",))
        em.op("vector",
              lambda e: e.scalar_tensor_tensor(out=_col(sm, 216), in0=_col(sm, cb1), scalar=-1.0,
                                               in1=_col(sm, cb1 + 3), op0=OP.mult, op1=OP.mult),
              reads=(f"c{cb1}", f"c{cb1+3}"), writes=("c216",))
        em.op("scalar",
              lambda e: e.activation(out=xm, in_=xm,
                                     func=AF.Identity, scale=_col(sm, 215), bias=_col(sm, 216)),
              reads=(xname, "c215", "c216"), writes=(xname,))
        for no in range(2):
            pst = self.ps_o if no == 0 else self.ps_o2
            pname = "ps_o" if no == 0 else "ps_o2"
            em.op("vector",
                  lambda e, no=no, pst=pst: e.scalar_tensor_tensor(
                      out=xm[:, no * 512 : no * 512 + 512],
                      in0=pst[:, 0:512], scalar=invs2,
                      in1=xm[:, no * 512 : no * 512 + 512],
                      op0=OP.mult, op1=OP.add,
                      accum_out=_col(sm, 204 + no)),
                  reads=(pname, "c37", xname), writes=(xname, f"c{204+no}"))
        mti = 2 * s + c01
        c = lambda i: _col(sm, 206 + i)
        em.op("vector",
              lambda e: e.tensor_tensor(out=c(0), in0=_col(sm, 204), in1=_col(sm, 205), op=OP.add),
              reads=("c204", "c205"), writes=("c206",))
        em.op("scalar",
              lambda e: e.activation(out=self.u_tok[:, 0:D], in_=xm, func=AF.Square,
                                     accum_out=c(1)),
              reads=(xname,), writes=("u_tok", "c207"))
        em.op("vector", lambda e: e.tensor_scalar(out=c(0), in0=c(0), scalar1=1.0 / D,
                                                  scalar2=None, op0=OP.mult),
              reads=("c206",), writes=("c206",))
        em.op("scalar", lambda e: e.activation(out=c(2), in_=c(0), func=AF.Square),
              reads=("c206",), writes=("c208",))
        em.op("vector",
              lambda e: e.scalar_tensor_tensor(out=c(1), in0=c(1), scalar=1.0 / D, in1=c(2),
                                               op0=OP.mult, op1=OP.subtract),
              reads=("c207", "c208"), writes=("c207",))
        self._rsqrt_col(c(3), c(1), EPS_LN, ("c207",), "c209", tmp=123)
        em.op("vector",
              lambda e: e.tensor_copy(out=_col(self.ffcol, 2 * mti + 1), in_=c(3)),
              reads=("c209",), writes=("ffcol",))
        em.op("vector",
              lambda e: e.tensor_tensor(out=_col(self.ffcol, 2 * mti), in0=c(0), in1=c(3),
                                        op=OP.mult),
              reads=("c206", "c209"), writes=("ffcol",))
        em.op("vector",
              lambda e: e.scalar_tensor_tensor(out=c(2), in0=c(0), scalar=-1.0, in1=c(3),
                                               op0=OP.mult, op1=OP.mult),
              reads=("c206", "c209"), writes=("c208",))
        em.op("scalar",
              lambda e: e.activation(out=self.u_tok[:, 0:D], in_=xm, func=AF.Abs,
                                     scale=c(3), bias=c(2)),
              reads=(xname, "c209", "c208"), writes=("u_tok",))
        em.op("vector",
              lambda e: e.tensor_reduce(out=c(4), in_=self.u_tok[:, 0:D],
                                        axis=mybir.AxisListType.X, op=OP.max),
              reads=("u_tok",), writes=("c210",))
        em.op("vector",
              lambda e: e.tensor_tensor(out=_col(sm, 200), in0=_col(sm, 200), in1=c(4), op=OP.max),
              reads=("c200", "c210"), writes=("c200",))
        em.dma("gpsimd", "o", self.d_xmid[(2 * s + c01) * 128 : (2 * s + c01 + 1) * 128, :],
               xm, reads=(xname,))

    def _out_prep(self, c01):
        em = self.em
        sm = self.small
        cb = 26
        c = lambda i: _col(sm, cb + i)
        cn = lambda i: f"c{cb + i}"
        y = self.y_sb
        scr = self.g_sb
        scrn = "g_sb"
        scr2 = self.bcst[:, 0:2048]
        scr2n = ("bch0", "bch1")
        em.op("scalar",
              lambda e: e.activation(out=scr, in_=y, func=AF.Square, accum_out=c(0)),
              reads=("y_sb",), writes=(scrn, cn(0)))
        em.op("vector", lambda e: e.tensor_scalar(out=c(0), in0=c(0), scalar1=1.0 / DIN,
                                                  scalar2=None, op0=OP.mult),
              reads=(cn(0),), writes=(cn(0),))
        self._rsqrt_col(c(1), c(0), EPS_RMS, (cn(0),), cn(1), tmp=122)
        em.op("vector",
              lambda e: e.scalar_tensor_tensor(out=scr, in0=y, scalar=c(1), in1=self.bc_w2,
                                               op0=OP.mult, op1=OP.mult, accum_out=c(2)),
              reads=("y_sb", cn(1), "bc_w2"), writes=(scrn, cn(2)))
        em.op("scalar",
              lambda e: e.activation(out=scr2, in_=scr, func=AF.Square, accum_out=c(3)),
              reads=(scrn,), writes=scr2n + (cn(3),))
        em.op("vector", lambda e: e.tensor_scalar(out=c(2), in0=c(2), scalar1=1.0 / DIN,
                                                  scalar2=None, op0=OP.mult),
              reads=(cn(2),), writes=(cn(2),))
        em.op("scalar", lambda e: e.activation(out=c(4), in_=c(2), func=AF.Square),
              reads=(cn(2),), writes=(cn(4),))
        em.op("vector",
              lambda e: e.scalar_tensor_tensor(out=c(3), in0=c(3), scalar=1.0 / DIN, in1=c(4),
                                               op0=OP.mult, op1=OP.subtract),
              reads=(cn(3), cn(4)), writes=(cn(3),))
        self._rsqrt_col(c(5), c(3), EPS_LN, (cn(3),), cn(5), tmp=122)
        em.op("vector",
              lambda e: e.tensor_reduce(out=c(4), in_=scr, axis=mybir.AxisListType.X, op=OP.max),
              reads=(scrn,), writes=(cn(4),))
        em.op("vector",
              lambda e: e.tensor_reduce(out=c(6), in_=scr, axis=mybir.AxisListType.X, op=OP.min),
              reads=(scrn,), writes=(cn(6),))
        em.op("vector", lambda e: e.tensor_tensor(out=c(4), in0=c(4), in1=c(2), op=OP.subtract),
              reads=(cn(4), cn(2)), writes=(cn(4),))
        em.op("vector", lambda e: e.tensor_tensor(out=c(6), in0=c(2), in1=c(6), op=OP.subtract),
              reads=(cn(6), cn(2)), writes=(cn(6),))
        em.op("vector", lambda e: e.tensor_tensor(out=c(4), in0=c(4), in1=c(6), op=OP.max),
              reads=(cn(4), cn(6)), writes=(cn(4),))
        em.op("vector", lambda e: e.tensor_tensor(out=c(4), in0=c(4), in1=c(5), op=OP.mult),
              reads=(cn(4), cn(5)), writes=(cn(4),))
        em.op("vector", lambda e: e.tensor_scalar(out=c(4), in0=c(4), scalar1=1e-5,
                                                  scalar2=None, op0=OP.max),
              reads=(cn(4),), writes=(cn(4),))
        em.op("vector", lambda e: e.tensor_scalar(out=c(10), in0=c(4), scalar1=1.0 / 127.0,
                                                  scalar2=None, op0=OP.mult),
              reads=(cn(4),), writes=(cn(10),))
        em.op("vector", lambda e: e.reciprocal(out=c(11), in_=c(10)),
              reads=(cn(10),), writes=(cn(11),))
        em.op("vector", lambda e: e.tensor_tensor(out=c(5), in0=c(5), in1=c(11), op=OP.mult),
              reads=(cn(5), cn(11)), writes=(cn(5),))
        em.op("vector",
              lambda e: e.scalar_tensor_tensor(out=c(6), in0=c(2), scalar=-1.0, in1=c(5),
                                               op0=OP.mult, op1=OP.mult),
              reads=(cn(2), cn(5)), writes=(cn(6),))
        em.op("scalar",
              lambda e: e.activation(out=scr2, in_=scr, func=AF.Identity, scale=c(5), bias=c(6)),
              reads=(scrn, cn(5), cn(6)), writes=scr2n)
        self._pe_round_T(scr2, scr2n, 16,
                         lambda k: self.yqT[:, k * CH : (k + 1) * CH], "yqT")
        em.op("vector", lambda e: e.tensor_copy(out=_col(sm, 37), in_=c(10)),
              reads=(cn(10),), writes=("c37",))

    def _ffn_scale_collective(self):
        em = self.em
        sm = self.small
        w1v = self.wq_in[:, 0 : 8 * DFF]
        em.dma("sync", "wl0", w1v[:, 0 : 4 * DFF].rearrange("p (k n) -> p k n", k=4),
               self.d_w1q.rearrange("(k p) n -> p k n", p=128)[:, 0:4],
               writes=("w1va", "wq_in"))
        em.dma("sync", "wl1", w1v[:, 4 * DFF : 8 * DFF].rearrange("p (k n) -> p k n", k=4),
               self.d_w1q.rearrange("(k p) n -> p k n", p=128)[:, 4:8],
               writes=("w1vb", "wq_in"))
        self._fold_partitions(_col(sm, 200), "c200", OP.max, 201, "c201")
        em.dma("gpsimd", "o", self.d_cc1i, sm[0:1, 201:202], reads=("c201",), writes=("cc1in",))
        em.op("gpsimd",
              lambda e: e.collective_compute(
                  "AllReduce", OP.max, replica_groups=[list(range(8))],
                  ins=[self.d_cc1i.opt()], outs=[self.d_cc1o.opt()]),
              reads=("cc1in",), writes=("cc1",))
        em.dma("sync", "b", _col(self.ffs, 0).rearrange("p (a f) -> p a f", a=1),
               self.d_cc1o.unsqueeze(0).to_broadcast((128, 1, 1)),
               reads=("cc1",), writes=("ffs0",))
        em.op("vector",
              lambda e: e.tensor_scalar(out=_col(self.ffs, 1), in0=_col(self.ffs, 0),
                                        scalar1=1e-6, scalar2=1.0 / 127.0,
                                        op0=OP.max, op1=OP.mult),
              reads=("ffs0",), writes=("ffs1",))
        em.op("vector", lambda e: e.reciprocal(out=_col(self.ffs, 2), in_=_col(self.ffs, 1)),
              reads=("ffs1",), writes=("ffs2",))
        em.op("vector",
              lambda e: e.tensor_tensor(out=_col(self.ffs, 3), in0=_col(self.ffs, 1),
                                        in1=_col(sm, 42), op=OP.mult),
              reads=("ffs1", "smallc42"), writes=("ffs3",))
        em.op("vector",
              lambda e: e.tensor_scalar(out=sm[:, 320 : 320 + 32], in0=sm[:, 320 : 320 + 32],
                                        scalar1=_col(self.ffs, 3), scalar2=None, op0=OP.mult),
              reads=("smallB1", "ffs3"), writes=("smallB1",))

    def _ffn_phase(self):
        em = self.em
        nc = self.nc
        sm = self.small
        em.phase_barrier()
        w1v = self.wq_in[:, 0 : 8 * DFF]
        em.op("gpsimd", lambda e: e.memset(_col(sm, 223), 0.0), writes=("c223",))

        def xload_f(fs):
            p = fs % 2
            dst = self.x4 if p == 0 else self.xbc[:, 0:4096]
            em.dma("sync", f"fx{p}",
                   dst.rearrange("p (i d) -> p i d", i=4),
                   self.d_xmid[fs * SEG2 : (fs + 1) * SEG2, :].rearrange(
                       "(i p) d -> p i d", p=128),
                   writes=(f"xf{p}",))

        xload_f(0)
        for fs in range(T // SEG2):
            t0 = fs * SEG2
            if fs + 1 < T // SEG2:
                xload_f(fs + 1)
            xbuf = self.x4 if fs % 2 == 0 else self.xbc[:, 0:4096]
            xbn = f"xf{fs % 2}"
            for i in range(4):
                mti = fs * 4 + i
                xi = xbuf[:, i * D : (i + 1) * D]
                scr = self.u_tok
                em.op("vector",
                      lambda e, mti=mti: e.tensor_scalar(
                          out=_col(sm, 220), in0=_col(self.ffcol, 2 * mti + 1),
                          scalar1=_col(self.ffs, 2), scalar2=None, op0=OP.mult),
                      reads=("ffcol", "ffs2"), writes=("c220",))
                em.op("vector",
                      lambda e, mti=mti: e.tensor_scalar(
                          out=_col(sm, 221), in0=_col(self.ffcol, 2 * mti),
                          scalar1=_col(self.ffs, 2), scalar2=-1.0,
                          op0=OP.mult, op1=OP.mult),
                      reads=("ffcol", "ffs2"), writes=("c221",))

                em.op("scalar",
                      lambda e, xi=xi, scr=scr: e.activation(out=scr[:, 0:D], in_=xi, func=AF.Identity,
                                                    scale=_col(sm, 220), bias=_col(sm, 221)),
                      reads=(xbn, "c220", "c221"), writes=("u_tok",))
                self._pe_round_T(
                    scr[:, 0:D], ("u_tok",), 8,
                    (lambda i: lambda k:
                     self.aq1T[:, k * SEG2 + i * 128 : k * SEG2 + i * 128 + 128])(i),
                    "aq1T")
            for n in range(DFF // 128):
                pst = self.ps_o if n % 2 == 0 else self.ps_o2
                pname = "ps_o" if n % 2 == 0 else "ps_o2"
                for k in range(8):
                    em.op("tensor",
                          lambda e, n=n, k=k, pst=pst: e.matmul(
                              pst[:, 0:512],
                              w1v[:, k * DFF + n * 128 : k * DFF + n * 128 + 128],
                              self.aq1T[:, k * SEG2 : (k + 1) * SEG2],
                              start=(k == 0), stop=(k == 7)),
                          reads=("w1va", "w1vb", "aq1T"), writes=(pname,))
                hn = f"hsb{n % 2}"
                hsb = self.y_sb if n % 2 == 0 else self.state
                em.op("scalar",
                      lambda e, n=n, pst=pst, hsb=hsb: e.activation(
                          out=hsb[:, 0:512], in_=pst[:, 0:512], func=AF.Gelu,
                          scale=_col(self.ffs, 3), bias=_col(sm, 320 + n)),
                      reads=(pname, "ffs3", "smallB1"), writes=(hn,))
                em.op("vector",
                      lambda e, hsb=hsb: e.tensor_reduce(
                          out=_col(sm, 222), in_=hsb[:, 0:512],
                          axis=mybir.AxisListType.X, op=OP.max, apply_absolute_value=True),
                      reads=(hn,), writes=("c222",))
                em.op("vector",
                      lambda e: e.tensor_tensor(out=_col(sm, 223), in0=_col(sm, 223),
                                                in1=_col(sm, 222), op=OP.max),
                      reads=("c223", "c222"), writes=("c223",))
                em.dma("gpsimd", "h", self.d_h[n * 128 : (n + 1) * 128, t0 : t0 + SEG2],
                       hsb[:, 0:512], reads=(hn,), writes=("d_hbuf",))

        w2v = self.wq_in[:, 0 : 32 * D]
        em.dma("sync", "wl0", w2v[:, 0 : 16 * D].rearrange("p (k n) -> p k n", k=16),
               self.d_w2q.rearrange("(k p) n -> p k n", p=128)[:, 0:16], writes=("w1va",))
        em.dma("sync", "wl1", w2v[:, 16 * D : 32 * D].rearrange("p (k n) -> p k n", k=16),
               self.d_w2q.rearrange("(k p) n -> p k n", p=128)[:, 16:32], writes=("w1vb",))
        em.dma("sync", "b", self.bc_w.rearrange("p (a f) -> p a f", a=1),
               self.d_b2.unsqueeze(0).to_broadcast((128, 1, D)), writes=("bc_w",))
        self._fold_partitions(_col(sm, 223), "c223", OP.max, 224, "c224")
        em.dma("gpsimd", "o", self.d_cc2i, sm[0:1, 224:225], reads=("c224",), writes=("cc2in",))
        em.op("gpsimd",
              lambda e: e.collective_compute(
                  "AllReduce", OP.max, replica_groups=[list(range(8))],
                  ins=[self.d_cc2i.opt()], outs=[self.d_cc2o.opt()]),
              reads=("cc2in",), writes=("cc2",))
        em.dma("sync", "b", _col(self.ffs, 4).rearrange("p (a f) -> p a f", a=1),
               self.d_cc2o.unsqueeze(0).to_broadcast((128, 1, 1)),
               reads=("cc2",), writes=("ffs4",))
        em.op("vector",
              lambda e: e.tensor_scalar(out=_col(self.ffs, 5), in0=_col(self.ffs, 4),
                                        scalar1=1e-6, scalar2=1.0 / 127.0,
                                        op0=OP.max, op1=OP.mult),
              reads=("ffs4",), writes=("ffs5",))
        em.op("vector", lambda e: e.reciprocal(out=_col(self.ffs, 6), in_=_col(self.ffs, 5)),
              reads=("ffs5",), writes=("ffs6",))
        em.op("vector",
              lambda e: e.tensor_tensor(out=_col(self.ffs, 7), in0=_col(self.ffs, 5),
                                        in1=_col(sm, 45), op=OP.mult),
              reads=("ffs5", "smallc45"), writes=("ffs7",))

        em.op("vector",
              lambda e: e.tensor_scalar(out=self.bc_w, in0=self.bc_w,
                                        scalar1=_col(self.ffs, 7), scalar2=None, op0=OP.mult),
              reads=("bc_w", "ffs7"), writes=("bc_w",))

        for fs in range(T // SEG2):
            t0 = fs * SEG2
            em.dma("sync", "x",
                   self.x4.rearrange("p (i d) -> p i d", i=4),
                   self.d_xmid[t0 : t0 + SEG2, :].rearrange("(i p) d -> p i d", p=128),
                   writes=("xseg",))
            for i in range(4):
                hraw = self.bcst
                for hf in range(2):
                    em.dma("sync", "h2",
                           hraw.rearrange("p (k t) -> p k t", k=16),
                           self.d_h.rearrange("(k p) t -> p k t", p=128)[
                               :, hf * 16 : (hf + 1) * 16, t0 + i * 128 : t0 + (i + 1) * 128],
                           reads=("d_hbuf",), writes=("bch0", "bch1"))
                    em.op("scalar",
                          lambda e: e.activation(
                              out=self.y_sb, in_=hraw,
                              func=AF.Identity, scale=_col(self.ffs, 6), bias=_col(self.consts, 3)),
                          reads=("bch0", "bch1", "ffs6", "rc_col"), writes=("hsb0",))
                    em.op("vector",
                          lambda e, hf=hf: e.tensor_scalar(
                              out=self.hq[:, hf * 2048 : (hf + 1) * 2048], in0=self.y_sb,
                              scalar1=-RC, scalar2=None, op0=OP.add),
                          reads=("hsb0",), writes=(f"hq{hf}",))
                for no in range(2):
                    pst = self.ps_o if no == 0 else self.ps_o2
                    pname = "ps_o" if no == 0 else "ps_o2"
                    for k in range(32):
                        em.op("tensor",
                              lambda e, k=k, no=no, pst=pst: e.matmul(
                                  pst[:, 0:512], self.hq[:, k * 128 : (k + 1) * 128],
                                  w2v[:, k * D + no * 512 : k * D + no * 512 + 512],
                                  start=(k == 0), stop=(k == 31)),
                              reads=(f"hq{k // 16}", "w1va", "w1vb"), writes=(pname,))
                xi = self.x4[:, i * D : (i + 1) * D]
                em.op("vector",
                      lambda e, xi=xi: e.tensor_tensor(out=xi, in0=xi, in1=self.bc_w, op=OP.add),
                      reads=("xseg", "bc_w"), writes=("xseg",))
                for no in range(2):
                    pst = self.ps_o if no == 0 else self.ps_o2
                    pname = "ps_o" if no == 0 else "ps_o2"
                    em.op("vector",
                          lambda e, xi=xi, no=no, pst=pst: e.scalar_tensor_tensor(
                              out=xi[:, no * 512 : no * 512 + 512], in0=pst[:, 0:512],
                              scalar=_col(self.ffs, 7), in1=xi[:, no * 512 : no * 512 + 512],
                              op0=OP.mult, op1=OP.add),
                          reads=(pname, "ffs7", "xseg"), writes=("xseg",))
                em.dma("gpsimd", "o", self.d_out[t0 + i * 128 : t0 + (i + 1) * 128, :],
                       xi, reads=("xseg",))

    def _assemble(self):
        nc = self.nc
        em = self.em
        with self.stack:
            em.finalize_sems(self.stack)
            with nc.Block() as blk:

                @blk.tensor
                def _(e):
                    em.replay("tensor", e)

                @blk.vector
                def _(e):
                    em.replay("vector", e)

                @blk.scalar
                def _(e):
                    em.replay("scalar", e)

                @blk.gpsimd
                def _(e):
                    em.replay("gpsimd", e)

                @blk.sync
                def _(e):
                    em.replay("sync", e)


_NC_CACHE = {}


def _build(stop_after=None):
    key = stop_after
    if key not in _NC_CACHE:
        nc = bass.Bass("TRN2", target_bir_lowering=False)
        Build(nc, stop_after=stop_after)
        _NC_CACHE[key] = nc
    return _NC_CACHE[key]


def make_in_map(inputs, core):
    x = np.ascontiguousarray(inputs["L0_attn_hidden_states"][core]).astype(np.float32)
    m = {
        "x": x,
        "norm_w": np.ascontiguousarray(inputs["L0_attn_norm_w"]).reshape(1, D),
        "win_t": np.ascontiguousarray(np.asarray(inputs["L0_attn_in_proj_w"]).T),
        "conv_w": np.ascontiguousarray(inputs["L0_attn_conv_w"]),
        "conv_b": np.ascontiguousarray(inputs["L0_attn_conv_b"]).reshape(CONVD, 1),
        "a_log": np.ascontiguousarray(inputs["L0_attn_A_log"]).reshape(H, 1),
        "dp": np.ascontiguousarray(inputs["L0_attn_Dp"]).reshape(1, H),
        "dt_bias": np.ascontiguousarray(inputs["L0_attn_dt_bias"]).reshape(H, 1),
        "out_norm_w": np.ascontiguousarray(inputs["L0_attn_out_norm_w"]).reshape(1, DIN),
        "wout_t": np.ascontiguousarray(np.asarray(inputs["L0_attn_out_proj_w"]).T),
        "w1_t": np.ascontiguousarray(np.asarray(inputs["L0_ffn_w1"]).T),
        "b1": np.ascontiguousarray(inputs["L0_ffn_b1"]).reshape(DFF, 1),
        "w2_t": np.ascontiguousarray(np.asarray(inputs["L0_ffn_w2"]).T),
        "b2": np.ascontiguousarray(inputs["L0_ffn_b2"]).reshape(1, D),
    }
    return {k: np.ascontiguousarray(np.asarray(v, dtype=np.float32)) for k, v in m.items()}


def kernel(**inputs):
    nc = _build()
    in_maps = [make_in_map(inputs, c) for c in range(8)]
    res = run_bass_kernel_spmd(nc, in_maps, core_ids=list(range(8)))
    out = np.stack([np.asarray(r["out"]) for r in res.results], axis=0)
    return out.astype(np.float32)
